# version 02
# baseline (speedup 1.0000x reference)
import sys
sys.path.insert(0, "/opt/trn_rl_repo")
import contextlib
import numpy as np
import concourse.bass as bass
import concourse.mybir as mybir
from concourse.bass_utils import run_bass_kernel_spmd

F32 = mybir.dt.float32
BF16 = mybir.dt.bfloat16
I32 = mybir.dt.int32
AF = mybir.ActivationFunctionType
ALU = mybir.AluOpType
AX = mybir.AxisListType

NCORES = 8
MAGIC = 1337


class Cfg:
    def __init__(self, B=2, S=4096, D=2048, H=16):
        self.B, self.S, self.D, self.H = B, S, D, H
        self.HD = D // H
        assert self.HD == 128 and S % 2048 == 0
        self.OC = min(512, D)
        self.DC = D // 128
        self.QTOK = S // 4
        self.KTOK = S // 2
        self.NQC = self.QTOK // 512
        self.NKC = S // 128
        self.NQT = self.QTOK // 128
        self.RANK, self.NE, self.RH = 16, 16, 64
        self.ER = self.RANK * self.NE


REAL = Cfg()


def make_masks(cprime):
    import ml_dtypes
    kk = np.arange(128)[:, None]
    jj = np.arange(128)[None, :]
    m = np.stack([(128 * w4 + kk <= 4 * jj + cprime) for w4 in range(4)])
    return np.ascontiguousarray(m.astype(ml_dtypes.bfloat16))


def prep_inputs(cfg, x, Wqkv, q_ln_scale, k_ln_scale, out_W,
                W, bias, rw1, rb1, rw2, rb2, A, B_, gates):
    B, S, D = cfg.B, cfg.S, cfg.D
    x = np.asarray(x)
    A = np.asarray(A); B_ = np.asarray(B_)
    AT = np.ascontiguousarray(A.transpose(2, 0, 1).reshape(D, cfg.ER))
    BT = np.ascontiguousarray(B_.transpose(0, 2, 1).reshape(cfg.ER, D))
    WT = np.ascontiguousarray(np.asarray(W).T)
    rw1T = np.ascontiguousarray(np.asarray(rw1).T)
    rw2T = np.ascontiguousarray(np.asarray(rw2).T[:, :cfg.NE])
    import ml_dtypes
    ident = np.eye(128, dtype=ml_dtypes.bfloat16)
    in_maps = []
    for c in range(NCORES):
        b, cp = c // 4, c % 4
        kvh = cp % 2
        qidx = np.arange(cp, S, 4)
        kvs = slice(kvh * (S // 2), (kvh + 1) * (S // 2))
        xb = x[b]
        in_maps.append({
            "x_qT": np.ascontiguousarray(xb[qidx].T),
            "x_kvT": np.ascontiguousarray(xb[kvs].T),
            "x_q": np.ascontiguousarray(xb[qidx]),
            "Wqkv": np.asarray(Wqkv),
            "outW": np.asarray(out_W),
            "WT": WT,
            "rw1T": rw1T,
            "rw2T": rw2T,
            "AT": AT,
            "BT": BT,
            "qls": np.asarray(q_ln_scale).reshape(1, D),
            "kls": np.asarray(k_ln_scale).reshape(1, D),
            "fbias": np.asarray(bias).reshape(1, D),
            "rb1": np.asarray(rb1).reshape(1, cfg.RH),
            "rb2": np.ascontiguousarray(np.asarray(rb2)[:cfg.NE].reshape(1, cfg.NE)),
            "gates": np.asarray(gates).reshape(1, cfg.NE),
            "masks": make_masks(cp),
            "ident": ident,
            "kvhalf": np.array([[kvh]], np.int32),
        })
    return in_maps


def build_nc(cfg, detect_races=False, debug_taps=False):
    nc = bass.Bass("TRN2", target_bir_lowering=False, num_devices=NCORES,
                   detect_race_conditions=detect_races,
                   dynamic_dma_scratch_size=16384)
    D, DC, H, S = cfg.D, cfg.DC, cfg.H, cfg.S
    QTOK, KTOK, NQC, NKC, NQT = cfg.QTOK, cfg.KTOK, cfg.NQC, cfg.NKC, cfg.NQT
    ER, NE, RH, RANK = cfg.ER, cfg.NE, cfg.RH, cfg.RANK
    OC = cfg.OC

    es = contextlib.ExitStack()
    sems, cnt = {}, {}

    def sem(n):
        sems[n] = es.enter_context(nc.semaphore(n))
        cnt[n] = 0

    for n in ("hs", "gs", "pe", "dv", "ac", "po", "ga", "gb", "ha", "hb", "sa", "sb", "ka", "kb", "va", "vb"):
        sem(n)

    def inc(instr, n, v=1):
        instr.then_inc(sems[n], v)
        cnt[n] += v
        return cnt[n]

    def hdma(dst, src, s="hs"):
        return inc(nc.sync.dma_start(dst, src), s, 16)

    def gdma(dst, src, cond=None, s="gs"):
        if cond is None:
            return inc(nc.gpsimd.dma_start(dst, src), s, 16)
        return inc(nc.gpsimd.dma_start(dst, src, cond=cond), s, 16)

    def w(engine, n, v=None):
        engine.wait_ge(sems[n], cnt[n] if v is None else v)

    def gdma_nc(dst, src, s="gs"):
        nc.gpsimd.dma_start(dst, src).then_inc(sems[s], 16)

    bar = nc.all_engine_barrier

    _WRAP = {"tensor_tensor", "tensor_scalar", "tensor_copy", "reduce_max",
             "reduce_sum", "reciprocal", "memset", "tensor_single_scalar",
             "activation", "select", "tensor_add", "tensor_mul", "tensor_sub"}

    class _Proxy:
        def __init__(self, eng, s):
            self._e, self._s = eng, s

        def __getattr__(self, name):
            attr = getattr(self._e, name)
            if name in _WRAP:
                def f(*a, **k):
                    self._e.wait_ge(sems[self._s], cnt[self._s])
                    r = attr(*a, **k)
                    r.then_inc(sems[self._s], 1)
                    cnt[self._s] += 1
                    return r
                return f
            return attr

    TEN, GPS, SYN = nc.tensor, nc.gpsimd, nc.sync
    VEC = _Proxy(nc.vector, "dv")
    SCA = _Proxy(nc.scalar, "ac")

    dt = lambda n, s, d=F32, kind="ExternalInput", **kw: \
        nc.dram_tensor(n, s, d, kind=kind, **kw)

    x_qT = dt("x_qT", [D, QTOK])
    x_kvT = dt("x_kvT", [D, KTOK])
    x_q = dt("x_q", [QTOK, D])
    Wqkv = dt("Wqkv", [D, 3 * D])
    outW = dt("outW", [D, D])
    WT_d = dt("WT", [D, D])
    rw1T = dt("rw1T", [D, RH])
    rw2T = dt("rw2T", [RH, NE])
    AT_d = dt("AT", [D, ER])
    BT_d = dt("BT", [ER, D])
    qls_d = dt("qls", [1, D])
    kls_d = dt("kls", [1, D])
    fbias = dt("fbias", [1, D])
    rb1_d = dt("rb1", [1, RH])
    rb2_d = dt("rb2", [1, NE])
    gates_d = dt("gates", [1, NE])
    masks_d = dt("masks", [4, 128, 128], BF16)
    ident_d = dt("ident", [128, 128], BF16)
    kvhalf_d = dt("kvhalf", [1, 1], I32)
    out_d = dt("out", [QTOK, D], kind="ExternalOutput")
    if debug_taps:
        dbg_ffn = dt("dbg_ffn", [NQT * 128, D], kind="ExternalOutput")
        dbg_ctx = dt("dbg_ctx", [D, QTOK], kind="ExternalOutput")
        dbg_qT = dt("dbg_qT", [D, QTOK], kind="ExternalOutput")
        dbg_shift = dt("dbg_shift", [128, H], kind="ExternalOutput")

    KT_g = dt("KT_g", [D, S], BF16, kind="Internal", addr_space="Shared")
    V_g = dt("V_g", [S, D], BF16, kind="Internal", addr_space="Shared")
    KN_g = dt("KN_g", [H, S], F32, kind="Internal", addr_space="Shared")
    FLG = dt("FLG", [2, 1], I32, kind="Internal", addr_space="Shared")
    scr = dt("scr", [1, 64], F32, kind="Internal")
    KT_l = dt("KT_l", [D, KTOK], BF16, kind="Internal")
    V_l = dt("V_l", [KTOK, D], BF16, kind="Internal")
    KN_l = dt("KN_l", [H, KTOK], F32, kind="Internal")

    qT = es.enter_context(nc.sbuf_tensor("qT", [128, DC, QTOK], BF16))
    ffn_s = es.enter_context(nc.sbuf_tensor("ffn_s", [128, NQT, D], BF16))
    ones_cb = es.enter_context(nc.sbuf_tensor("ones_cb", [128, 1], BF16))
    ones_rb = es.enter_context(nc.sbuf_tensor("ones_rb", [1, 128], BF16))
    ones_cf = es.enter_context(nc.sbuf_tensor("ones_cf", [128, 1], F32))
    ident = es.enter_context(nc.sbuf_tensor("ident_s", [128, 128], BF16))
    masks = es.enter_context(nc.sbuf_tensor("masks_s", [128, 4, 128], BF16))
    kvh_s = es.enter_context(nc.sbuf_tensor("kvh_s", [1, 1], I32))
    brow = es.enter_context(nc.sbuf_tensor("brow", [1, NE], BF16))
    rb1_s = es.enter_context(nc.sbuf_tensor("rb1_s", [1, RH], BF16))
    shifts = es.enter_context(nc.sbuf_tensor("shifts", [128, H], F32))
    qmax_r = es.enter_context(nc.sbuf_tensor("qmax_r", [1, H], F32))
    flagv = es.enter_context(nc.sbuf_tensor("flagv", [1, 1], I32))
    magic_s = es.enter_context(nc.sbuf_tensor("magic_s", [1, 1], I32))
    zero_s = es.enter_context(nc.sbuf_tensor("zero_s", [1, 1], I32))
    srow2 = es.enter_context(nc.sbuf_tensor("srow2", [1, 2 * NE], F32))
    eps5 = es.enter_context(nc.sbuf_tensor("eps5", [1, 1], F32))
    ones_rf = es.enter_context(nc.sbuf_tensor("ones_rf", [1, 128], F32))
    identF = es.enter_context(nc.sbuf_tensor("identF", [128, 128], F32))
    brow_f = es.enter_context(nc.sbuf_tensor("brow_f", [1, NE], F32))
    eps6 = es.enter_context(nc.sbuf_tensor("eps6", [1, 1], F32))

    gdma(kvh_s[:], kvhalf_d.ap())
    gdma(masks[:], masks_d.ap().rearrange("w p q -> p w q"))
    gdma(ident[:], ident_d.ap())
    gdma(srow2[0:1, 0:NE], rb2_d.ap())
    gdma(srow2[0:1, NE:2 * NE], gates_d.ap())
    gdma(rb1_s[:], rb1_d.ap())
    w(GPS, "gs")
    GPS.memset(ones_cb[:], 1.0)
    GPS.memset(ones_rb[:], 1.0)
    GPS.memset(ones_cf[:], 1.0)
    GPS.memset(ones_rf[:], 1.0)
    GPS.memset(magic_s[:], MAGIC)
    GPS.memset(eps5[:], 1e-5)
    GPS.memset(eps6[:], 1e-6)
    GPS.memset(zero_s[:], 0)
    hv = GPS.value_load(kvh_s[0:1, 0:1])
    bar()
    with GPS.If(hv == 0):
        gdma(FLG.ap()[0:1, :], zero_s[:])
    with GPS.Else():
        gdma_nc(FLG.ap()[1:2, :], zero_s[:])
    w(GPS, "gs")
    VEC.tensor_tensor(brow_f[:], srow2[0:1, 0:NE], srow2[0:1, NE:2 * NE],
                      op=ALU.add)
    VEC.tensor_copy(identF[:], ident[:])
    bar()

    def ln_fm(xsrc, t0, T, dst, eps_t, tag, dst_err=None):
        NT = T // 512
        with contextlib.ExitStack() as st:
            xa = st.enter_context(nc.sbuf_tensor(f"lnx_{tag}", [128, 2, T], F32))
            xsq = st.enter_context(nc.sbuf_tensor(f"lnq_{tag}", [128, T], F32))
            mrow = st.enter_context(nc.sbuf_tensor(f"lnm_{tag}", [1, T], F32))
            vrow = st.enter_context(nc.sbuf_tensor(f"lnv_{tag}", [1, T], F32))
            trow = st.enter_context(nc.sbuf_tensor(f"lnt_{tag}", [1, T], F32))
            mrow_b = st.enter_context(nc.sbuf_tensor(f"lnmb_{tag}", [1, T], BF16))
            rrow_b = st.enter_context(nc.sbuf_tensor(f"lnrb_{tag}", [1, T], BF16))
            mb = st.enter_context(nc.sbuf_tensor(f"lnM_{tag}", [128, T], BF16))
            rb = st.enter_context(nc.sbuf_tensor(f"lnR_{tag}", [128, T], BF16))
            ps_s = st.enter_context(nc.psum_tensor(f"lns_{tag}", [1, T], F32))
            ps_q = st.enter_context(nc.psum_tensor(f"lnp_{tag}", [1, T], F32))
            ps_b = st.enter_context(nc.psum_tensor(f"lnw_{tag}", [128, 512], F32))
            pe0 = cnt["pe"]
            for dc in range(DC):
                if dc >= 2:
                    w(SYN, "pe", pe0 + dc - 1)
                sl = ("ha", "hb")[dc % 2]
                hdma(xa[:, dc % 2, :],
                     xsrc.ap()[:, t0:t0 + T].rearrange("(c p) t -> c p t", p=128)[dc],
                     s=sl)
                w(TEN, sl)
                w(SCA, sl)
                for t in range(NT):
                    TEN.matmul(ps_s[0:1, 512 * t:512 * t + 512], ones_cf[:],
                               xa[:, dc % 2, 512 * t:512 * t + 512],
                               start=(dc == 0), stop=(dc == DC - 1))
                if dc > 0:
                    w(SCA, "pe", pe0 + dc)
                SCA.activation(xsq[:], xa[:, dc % 2, :], AF.Square)
                w(TEN, "ac")
                for t in range(NT):
                    m = TEN.matmul(ps_q[0:1, 512 * t:512 * t + 512], ones_cf[:],
                                   xsq[:, 512 * t:512 * t + 512],
                                   start=(dc == 0), stop=(dc == DC - 1))
                    if t == NT - 1:
                        inc(m, "pe")
            bar()
            VEC.tensor_scalar(mrow[:], ps_s[0:1, :], 1.0 / D, None, op0=ALU.mult)
            VEC.tensor_scalar(vrow[:], ps_q[0:1, :], 1.0 / D, None, op0=ALU.mult)
            VEC.tensor_tensor(trow[:], mrow[:], mrow[:], op=ALU.mult)
            VEC.tensor_tensor(vrow[:], vrow[:], trow[:], op=ALU.subtract)
            w(SCA, "dv")
            SCA.activation(vrow[:], vrow[:], AF.Sqrt, bias=eps_t[0:1, :])
            w(VEC, "ac")
            VEC.reciprocal(vrow[:], vrow[:])
            VEC.tensor_copy(mrow_b[:], mrow[:])
            VEC.tensor_copy(rrow_b[:], vrow[:])
            w(TEN, "dv")
            for t in range(NT):
                for rsrc, dstb in ((mrow_b, mb), (rrow_b, rb)):
                    inc(TEN.matmul(ps_b[:, :], ones_rb[:],
                                   rsrc[0:1, 512 * t:512 * t + 512],
                                   start=True, stop=True), "pe")
                    w(VEC, "pe")
                    VEC.tensor_copy(dstb[:, 512 * t:512 * t + 512], ps_b[:, :])
                    w(TEN, "dv")
            bar()
            dv0 = cnt["dv"]
            opd = 2 if dst_err is None else 4
            for dc in range(DC):
                if dc >= 2:
                    w(SYN, "dv", dv0 + opd * (dc - 1))
                sl = ("ha", "hb")[dc % 2]
                hdma(xa[:, dc % 2, :],
                     xsrc.ap()[:, t0:t0 + T].rearrange("(c p) t -> c p t", p=128)[dc],
                     s=sl)
                w(VEC, sl)
                VEC.tensor_tensor(xsq[:], xa[:, dc % 2, :], mb[:], op=ALU.subtract)
                if dst_err is None:
                    VEC.tensor_tensor(dst[:, dc, :], xsq[:], rb[:], op=ALU.mult)
                else:
                    VEC.tensor_tensor(xsq[:], xsq[:], rb[:], op=ALU.mult)
                    VEC.tensor_copy(dst[:, dc, :], xsq[:])
                    VEC.tensor_tensor(dst_err[:, dc, :], xsq[:], dst[:, dc, :],
                                      op=ALU.subtract)
            bar()

    def proj_ln(cb, xn_src, T, scale_tile, eps_t, tag, sink, nrm_sink=None,
                rstd_scale=None):
        NT = T // 512
        with contextlib.ExitStack() as st:
            wb = st.enter_context(nc.sbuf_tensor(f"pw_{tag}", [128, 2, DC, 128], BF16))
            acc = st.enter_context(nc.sbuf_tensor(f"pa_{tag}", [128, DC, T], BF16))
            outc = st.enter_context(nc.sbuf_tensor(f"po_{tag}", [128, 2, T], BF16))
            sq = st.enter_context(nc.sbuf_tensor(f"pq_{tag}", [128, T], BF16))
            mrow = st.enter_context(nc.sbuf_tensor(f"pm_{tag}", [1, T], F32))
            vrow = st.enter_context(nc.sbuf_tensor(f"pv_{tag}", [1, T], F32))
            mrow_b = st.enter_context(nc.sbuf_tensor(f"pmb_{tag}", [1, T], BF16))
            rrow_b = st.enter_context(nc.sbuf_tensor(f"prb_{tag}", [1, T], BF16))
            mb = st.enter_context(nc.sbuf_tensor(f"pM_{tag}", [128, T], BF16))
            rsb = st.enter_context(nc.sbuf_tensor(f"pR_{tag}", [128, 2, T], BF16))
            nrow = st.enter_context(nc.sbuf_tensor(f"pn_{tag}", [1, T], F32))
            ps0 = st.enter_context(nc.psum_tensor(f"p0_{tag}", [128, 512], F32))
            ps1 = st.enter_context(nc.psum_tensor(f"p1_{tag}", [128, 512], F32))
            ps_s = st.enter_context(nc.psum_tensor(f"p2_{tag}", [1, T], F32))
            ps_q = st.enter_context(nc.psum_tensor(f"p3_{tag}", [1, T], F32))
            ps_b = st.enter_context(nc.psum_tensor(f"p4_{tag}", [128, 512], F32))
            psl = [ps0, ps1]
            pe_cc = []
            gdma(wb[:, 0], Wqkv.ap()[:, cb:cb + 128]
                 .rearrange("(c p) x -> p c x", p=128), s="ga")
            dv0, pe0 = cnt["dv"], cnt["pe"]
            gtile = 0
            for cc in range(DC):
                if cc + 1 < DC:
                    if cc >= 1:
                        w(GPS, "pe", pe_cc[cc - 1])
                    gdma(wb[:, (cc + 1) % 2],
                         Wqkv.ap()[:, cb + 128 * (cc + 1):cb + 128 * (cc + 1) + 128]
                         .rearrange("(c p) x -> p c x", p=128),
                         s=("ga", "gb")[(cc + 1) % 2])
                w(TEN, ("ga", "gb")[cc % 2])
                for t in range(NT):
                    p = psl[gtile % 2]
                    if gtile >= 2:
                        w(TEN, "dv", dv0 + gtile - 1)
                    for dc in range(DC):
                        m = TEN.matmul(p[:, :], wb[:, cc % 2, dc],
                                       xn_src[:, dc, 512 * t:512 * t + 512],
                                       start=(dc == 0), stop=(dc == DC - 1))
                    inc(m, "pe")
                    w(VEC, "pe")
                    VEC.tensor_copy(acc[:, cc, 512 * t:512 * t + 512], p[:, :])
                    gtile += 1
                w(SCA, "dv")
                if cc > 0:
                    w(SCA, "pe", cnt["pe"])
                SCA.activation(sq[:], acc[:, cc, :], AF.Square)
                w(TEN, "ac")
                for t in range(NT):
                    TEN.matmul(ps_s[0:1, 512 * t:512 * t + 512], ones_cb[:],
                               acc[:, cc, 512 * t:512 * t + 512],
                               start=(cc == 0), stop=(cc == DC - 1))
                    m = TEN.matmul(ps_q[0:1, 512 * t:512 * t + 512], ones_cb[:],
                                   sq[:, 512 * t:512 * t + 512],
                                   start=(cc == 0), stop=(cc == DC - 1))
                inc(m, "pe")
                pe_cc.append(cnt["pe"])
            bar()
            VEC.tensor_scalar(mrow[:], ps_s[0:1, :], 1.0 / D, None, op0=ALU.mult)
            VEC.tensor_scalar(vrow[:], ps_q[0:1, :], 1.0 / D, None, op0=ALU.mult)
            VEC.tensor_tensor(nrow[:], mrow[:], mrow[:], op=ALU.mult)
            VEC.tensor_tensor(vrow[:], vrow[:], nrow[:], op=ALU.subtract)
            w(SCA, "dv")
            SCA.activation(vrow[:], vrow[:], AF.Sqrt, bias=eps_t[0:1, :])
            w(VEC, "ac")
            VEC.reciprocal(vrow[:], vrow[:])
            if rstd_scale is not None:
                VEC.tensor_scalar(vrow[:], vrow[:], float(rstd_scale), None,
                                  op0=ALU.mult)
            VEC.tensor_copy(mrow_b[:], mrow[:])
            VEC.tensor_copy(rrow_b[:], vrow[:])
            w(TEN, "dv")
            for t in range(NT):
                inc(TEN.matmul(ps_b[:, :], ones_rb[:],
                               mrow_b[0:1, 512 * t:512 * t + 512],
                               start=True, stop=True), "pe")
                w(VEC, "pe")
                VEC.tensor_copy(mb[:, 512 * t:512 * t + 512], ps_b[:, :])
                w(TEN, "dv")
            bar()
            for dc in range(DC):
                pp = dc % 2
                for t in range(NT):
                    inc(TEN.matmul(ps_b[:, :],
                                   scale_tile[0:1, 128 * dc:128 * dc + 128],
                                   rrow_b[0:1, 512 * t:512 * t + 512],
                                   start=True, stop=True), "pe")
                    w(VEC, "pe")
                    VEC.tensor_copy(rsb[:, pp, 512 * t:512 * t + 512],
                                        ps_b[:, :])
                    w(TEN, "dv")
                VEC.tensor_tensor(sq[:], acc[:, dc, :], mb[:], op=ALU.subtract)
                VEC.tensor_tensor(outc[:, pp, :], sq[:], rsb[:, pp, :],
                                      op=ALU.mult)
                sink(dc, outc[:, pp, :])
                if nrm_sink is not None:
                    w(SCA, "dv")
                    SCA.activation(sq[:], outc[:, pp, :], AF.Square)
                    w(TEN, "ac")
                    for t in range(NT):
                        m = TEN.matmul(ps_s[0:1, 512 * t:512 * t + 512], ones_cb[:],
                                       sq[:, 512 * t:512 * t + 512],
                                       start=True, stop=True)
                    inc(m, "pe")
                    w(VEC, "pe")
                    VEC.tensor_copy(nrow[0:1, :], ps_s[0:1, :])
                    nrm_sink(dc, nrow[0:1, :])
            bar()

    xnq_ctx = contextlib.ExitStack()
    xnT_q = xnq_ctx.enter_context(nc.sbuf_tensor("xnT_q", [128, DC, QTOK], BF16))
    qls_s = xnq_ctx.enter_context(nc.sbuf_tensor("qls_s", [1, D], BF16))
    gdma(qls_s[:], qls_d.ap())
    w(GPS, "gs")
    xn_err = xnq_ctx.enter_context(nc.sbuf_tensor("xn_err", [128, DC, QTOK], BF16))
    ln_fm(x_qT, 0, QTOK, xnT_q, eps5, "q", dst_err=xn_err)

    def q_sink(dc, tile):
        VEC.tensor_copy(qT[:, dc, :], tile)

    def q_nrm(dc, row):
        VEC.reduce_max(qmax_r[0:1, dc:dc + 1], row, axis=AX.X)

    proj_ln(0, xnT_q, QTOK, qls_s, eps6, "q", q_sink, q_nrm,
            rstd_scale=1.0 / float(np.sqrt(128.0)))
    bar()

    with contextlib.ExitStack() as st:
        wxaT_a = st.enter_context(nc.sbuf_tensor("wxaT_a", [128, NQT, 2, 128], BF16))
        r1f = st.enter_context(nc.sbuf_tensor("f_r1f", [128, DC, RH], F32))
        r1b = st.enter_context(nc.sbuf_tensor("f_r1", [128, DC, RH], BF16))
        r1l = st.enter_context(nc.sbuf_tensor("f_r1l", [128, DC, RH], BF16))
        r2f = st.enter_context(nc.sbuf_tensor("f_r2", [RH, NE], F32))
        rb1f = st.enter_context(nc.sbuf_tensor("f_rb1f", [1, RH], F32))
        rb1lo = st.enter_context(nc.sbuf_tensor("f_rb1lo", [1, RH], BF16))
        atb = st.enter_context(nc.sbuf_tensor("f_at", [128, DC, ER], BF16))
        btb = st.enter_context(nc.sbuf_tensor("f_bt", [128, 2, D], BF16))
        bias_b = st.enter_context(nc.sbuf_tensor("f_bias", [1, D], BF16))
        hrow = st.enter_context(nc.sbuf_tensor("f_h", [128, RH], F32))
        hrowT = st.enter_context(nc.sbuf_tensor("f_hT", [RH, 128], F32))
        logi = st.enter_context(nc.sbuf_tensor("f_lg", [128, NE], F32))
        m1 = st.enter_context(nc.sbuf_tensor("f_m1", [128, 1], F32))
        m1n = st.enter_context(nc.sbuf_tensor("f_m1n", [128, 1], F32))
        m2 = st.enter_context(nc.sbuf_tensor("f_m2", [128, 1], F32))
        eq = st.enter_context(nc.sbuf_tensor("f_eq", [128, NE], F32))
        ee = st.enter_context(nc.sbuf_tensor("f_ee", [128, NE], F32))
        den = st.enter_context(nc.sbuf_tensor("f_den", [128, 1], F32))
        coeff = st.enter_context(nc.sbuf_tensor("f_cf", [128, NE], F32))
        crep = st.enter_context(nc.sbuf_tensor("f_cr", [128, ER], BF16))
        xa_s = st.enter_context(nc.sbuf_tensor("f_xa", [128, ER], F32))
        wxa = st.enter_context(nc.sbuf_tensor("f_wxa", [128, ER], BF16))
        FOC = min(256, OC)
        wfb = st.enter_context(nc.sbuf_tensor("f_w", [128, 2, DC, FOC], BF16))
        psh = st.enter_context(nc.psum_tensor("f_ph", [128, RH], F32))
        psg = st.enter_context(nc.psum_tensor("f_pg", [128, NE], F32))
        psx = st.enter_context(nc.psum_tensor("f_px", [128, ER], F32))
        pst = st.enter_context(nc.psum_tensor("f_pt", [128, 2, 128], BF16))
        psh2 = st.enter_context(nc.psum_tensor("f_ph2", [RH, 128], F32))
        psf0 = st.enter_context(nc.psum_tensor("f_p0", [128, 256 if OC >= 256 else OC], F32))
        psf1 = st.enter_context(nc.psum_tensor("f_p1", [128, 256 if OC >= 256 else OC], F32))
        psfl = [psf0, psf1]
        gdma(r1f[:], rw1T.ap().rearrange("(c p) r -> p c r", p=128))
        gdma(r2f[:], rw2T.ap())
        gdma(rb1f[:], rb1_d.ap())
        gdma(atb[:], AT_d.ap().rearrange("(c p) e -> p c e", p=128))
        gdma(btb[:], BT_d.ap().rearrange("(c p) e -> p c e", p=128))
        gdma(bias_b[:], fbias.ap())
        w(GPS, "gs")
        bar()
        VEC.tensor_copy(r1b[:], r1f[:])
        VEC.tensor_tensor(r1l[:], r1f[:], r1b[:], op=ALU.subtract)
        VEC.tensor_copy(rb1_s[:], rb1f[:])
        VEC.tensor_tensor(rb1lo[:], rb1f[:], rb1_s[:], op=ALU.subtract)
        bar()
        for tt in range(NQT):
            xt = xnT_q[:, :, 128 * tt:128 * tt + 128]
            xe = xn_err[:, :, 128 * tt:128 * tt + 128]
            for dc in range(DC):
                TEN.matmul(psh[:, :], xt[:, dc], r1b[:, dc],
                           start=(dc == 0), stop=False)
                TEN.matmul(psh[:, :], xt[:, dc], r1l[:, dc],
                           start=False, stop=False)
                TEN.matmul(psh[:, :], xe[:, dc], r1b[:, dc],
                           start=False, stop=False)
            TEN.matmul(psh[:, :], ones_rb[:], rb1_s[:],
                       start=False, stop=False)
            inc(TEN.matmul(psh[:, :], ones_rb[:], rb1lo[:],
                           start=False, stop=True), "pe")
            w(SCA, "pe")
            SCA.activation(hrow[:], psh[:, :], AF.Silu)
            w(TEN, "ac")
            inc(TEN.transpose(psh2[:, :], hrow[:, :], identF[:]), "pe")
            w(VEC, "pe")
            VEC.tensor_copy(hrowT[:, :], psh2[:, :])
            w(TEN, "dv")
            TEN.matmul(psg[:, :], hrowT[:, :], r2f[:], start=True, stop=False)
            inc(TEN.matmul(psg[:, :], ones_rf[:], brow_f[:],
                           start=False, stop=True), "pe")
            w(VEC, "pe")
            VEC.tensor_copy(logi[:], psg[:, :])
            VEC.reduce_max(m1[:], logi[:], axis=AX.X)
            VEC.tensor_scalar(m1n[:], m1[:], -1.0, None, op0=ALU.mult)
            VEC.tensor_scalar(eq[:], logi[:], m1[:], None, op0=ALU.is_ge)
            VEC.tensor_scalar(eq[:], eq[:], -1e30, None, op0=ALU.mult)
            VEC.tensor_tensor(eq[:], logi[:], eq[:], op=ALU.add)
            VEC.reduce_max(m2[:], eq[:], axis=AX.X)
            w(SCA, "dv")
            SCA.activation(ee[:], logi[:], AF.Exp, bias=m1n[:])
            w(VEC, "ac")
            VEC.tensor_scalar(eq[:], logi[:], m2[:], None, op0=ALU.is_ge)
            VEC.tensor_tensor(ee[:], ee[:], eq[:], op=ALU.mult)
            VEC.reduce_sum(den[:], ee[:], axis=AX.X)
            VEC.reciprocal(den[:], den[:])
            VEC.tensor_scalar(coeff[:], ee[:], den[:], None, op0=ALU.mult)
            for dc in range(DC):
                m = TEN.matmul(psx[:, :], xt[:, dc], atb[:, dc],
                               start=(dc == 0), stop=(dc == DC - 1))
            inc(m, "pe")
            w(VEC, "pe")
            VEC.tensor_copy(xa_s[:], psx[:, :])
            for r in range(RANK):
                VEC.tensor_copy(
                    crep[:].rearrange("p (e r) -> p e r", r=RANK)[:, :, r],
                    coeff[:])
            VEC.tensor_tensor(wxa[:], xa_s[:], crep[:], op=ALU.mult)
            w(TEN, "dv")
            for eh in range(2):
                m = TEN.transpose(pst[:, eh, :], wxa[:, 128 * eh:128 * eh + 128],
                                  ident[:])
            inc(m, "pe")
            w(VEC, "pe")
            VEC.tensor_copy(wxaT_a[:, tt, :, :], pst[:, :, :])
            w(TEN, "dv")
        bar()
        gdma(wfb[:, 0], WT_d.ap()[:, 0:FOC].rearrange("(c p) x -> p c x", p=128),
             s="ga")
        evac_dv = []
        pe_oc = []
        for oc in range(D // FOC):
            if oc + 1 < D // FOC:
                if oc >= 1:
                    w(GPS, "pe", pe_oc[oc - 1])
                gdma(wfb[:, (oc + 1) % 2],
                     WT_d.ap()[:, FOC * (oc + 1):FOC * (oc + 1) + FOC]
                     .rearrange("(c p) x -> p c x", p=128),
                     s=("ga", "gb")[(oc + 1) % 2])
            w(TEN, ("ga", "gb")[oc % 2])
            for tt in range(NQT):
                g = oc * NQT + tt
                p = psfl[g % 2]
                if g >= 2:
                    w(TEN, "dv", evac_dv[g - 2])
                xt = xnT_q[:, :, 128 * tt:128 * tt + 128]
                for dc in range(DC):
                    TEN.matmul(p[:, :], xt[:, dc], wfb[:, oc % 2, dc],
                               start=(dc == 0), stop=False)
                for eh in range(2):
                    TEN.matmul(p[:, :], wxaT_a[:, tt, eh, :],
                               btb[:, eh, FOC * oc:FOC * oc + FOC],
                               start=False, stop=False)
                inc(TEN.matmul(p[:, :], ones_rb[:],
                               bias_b[0:1, FOC * oc:FOC * oc + FOC],
                               start=False, stop=True), "pe")
                w(VEC, "pe")
                VEC.tensor_copy(ffn_s[:, tt, FOC * oc:FOC * oc + FOC], p[:, :])
                evac_dv.append(cnt["dv"])
            pe_oc.append(cnt["pe"])
        bar()
    xnq_ctx.close()
    bar()

    TKV = KTOK // 2
    with contextlib.ExitStack() as kv_ctx:
        xnT_kv = kv_ctx.enter_context(
            nc.sbuf_tensor("xnT_kv", [128, DC, TKV], BF16))
        kls_s = kv_ctx.enter_context(nc.sbuf_tensor("kls_s", [1, D], BF16))
        gdma(kls_s[:], kls_d.ap())
        w(GPS, "gs")
        VOC = min(256, OC)
        vw = kv_ctx.enter_context(nc.sbuf_tensor("vw", [128, 2, DC, VOC], BF16))
        vout = kv_ctx.enter_context(nc.sbuf_tensor("vout", [128, 2, VOC], BF16))
        for sh in range(2):
            t0 = sh * TKV
            ln_fm(x_kvT, t0, TKV, xnT_kv, eps5, f"kv{sh}")

            def k_sink(dc, tile, _t0=t0):
                w(GPS, "dv")
                gdma(KT_l.ap()[128 * dc:128 * dc + 128, _t0:_t0 + TKV], tile)
                w(VEC, "gs")

            def k_nrm(dc, row, _t0=t0):
                w(GPS, "dv")
                gdma(KN_l.ap()[dc:dc + 1, _t0:_t0 + TKV], row)
                w(VEC, "gs")

            proj_ln(D, xnT_kv, TKV, kls_s, eps6, f"k{sh}", k_sink, k_nrm)
            bar()
            NKT = TKV // 128
            with contextlib.ExitStack() as st:
                psv0 = st.enter_context(nc.psum_tensor(f"v0_{sh}", [128, VOC], F32))
                psv1 = st.enter_context(nc.psum_tensor(f"v1_{sh}", [128, VOC], F32))
                psvl = [psv0, psv1]
                gdma(vw[:, 0], Wqkv.ap()[:, 2 * D:2 * D + VOC]
                     .rearrange("(c p) x -> p c x", p=128), s="ga")
                evac_dv = []
                store_gs = []
                pe_cc = []
                for cc in range(D // VOC):
                    if cc + 1 < D // VOC:
                        if cc >= 1:
                            w(GPS, "pe", pe_cc[cc - 1])
                        gdma(vw[:, (cc + 1) % 2],
                             Wqkv.ap()[:, 2 * D + VOC * (cc + 1):
                                       2 * D + VOC * (cc + 1) + VOC]
                             .rearrange("(c p) x -> p c x", p=128),
                             s=("ga", "gb")[(cc + 1) % 2])
                    w(TEN, ("ga", "gb")[cc % 2])
                    for tt in range(NKT):
                        g = cc * NKT + tt
                        p = psvl[g % 2]
                        if g >= 2:
                            w(TEN, "dv", evac_dv[g - 2])
                        for dc in range(DC):
                            m = TEN.matmul(p[:, :],
                                           xnT_kv[:, dc, 128 * tt:128 * tt + 128],
                                           vw[:, cc % 2, dc],
                                           start=(dc == 0), stop=(dc == DC - 1))
                        inc(m, "pe")
                        w(VEC, "pe")
                        if g >= 2:
                            w(VEC, ("va", "vb")[g % 2], store_gs[g - 2])
                        VEC.tensor_copy(vout[:, g % 2, :], p[:, :])
                        evac_dv.append(cnt["dv"])
                        w(GPS, "dv")
                        gdma(V_l.ap()[t0 + 128 * tt:t0 + 128 * tt + 128,
                                      VOC * cc:VOC * cc + VOC],
                             vout[:, g % 2, :], s=("va", "vb")[g % 2])
                        store_gs.append(cnt[("va", "vb")[g % 2]])
                    pe_cc.append(cnt["pe"])
                    bar()
                w(GPS, "va")
                w(GPS, "vb")
                bar()
    bar()
    w(GPS, "gs")
    w(GPS, "ga")
    w(GPS, "gb")
    w(GPS, "va")
    w(GPS, "vb")
    PR = min(512, D)
    PV = min(1024, KTOK)

    def publish(hh, emit):
        for rr in range(0, D, PR):
            emit(KT_g.ap()[rr:rr + PR, hh * KTOK:(hh + 1) * KTOK],
                 KT_l.ap()[rr:rr + PR, :])
        for rr in range(0, KTOK, PV):
            emit(V_g.ap()[hh * KTOK + rr: hh * KTOK + rr + PV, :],
                 V_l.ap()[rr:rr + PV, :])
        emit(KN_g.ap()[:, hh * KTOK:(hh + 1) * KTOK], KN_l.ap())
        emit(FLG.ap()[hh:hh + 1, :], magic_s[:])

    with GPS.If(hv == 0):
        publish(0, lambda d_, s_: gdma(d_, s_))
    with GPS.Else():
        publish(1, lambda d_, s_: gdma_nc(d_, s_))
    w(GPS, "gs")
    bar()

    with nc.gpsimd.register("fl") as fl, nc.gpsimd.register("it") as it:
        GPS.reg_mov(fl, 0)
        GPS.reg_mov(it, 0)
        with nc.bb("kv_poll_check"):
            GPS.br_lt(fl, MAGIC, "kv_poll_body", "kv_poll_done")
        with nc.bb("kv_poll_body"):
            GPS.sem_clear(sems["po"])
            with GPS.If(hv == 0):
                GPS.dma_start(flagv[:], FLG.ap()[1:2, :]).then_inc(sems["po"], 16)
            with GPS.Else():
                GPS.dma_start(flagv[:], FLG.ap()[0:1, :]).then_inc(sems["po"], 16)
            GPS.wait_ge(sems["po"], 16)
            GPS.reg_load(fl, flagv[0:1, 0:1])
            GPS.reg_add(it, it, 1)
            GPS.br_lt(it, 2000000, "kv_poll_check", "kv_poll_done")
        with nc.bb("kv_poll_done"):
            GPS.nop()
    bar()

    with contextlib.ExitStack() as st:
        knrow = st.enter_context(nc.sbuf_tensor("s_kn", [H, S], F32))
        kmax_c = st.enter_context(nc.sbuf_tensor("s_kc", [H, 1], F32))
        crow = st.enter_context(nc.sbuf_tensor("s_cr", [1, H], F32))
        crow_b = st.enter_context(nc.sbuf_tensor("s_cb", [1, H], BF16))
        psb2 = st.enter_context(nc.psum_tensor("s_ps", [128, H], F32))
        hdma(knrow[:], KN_g.ap())
        w(VEC, "hs")
        VEC.reduce_max(kmax_c[:, 0:1], knrow[:], axis=AX.X)
        w(GPS, "dv")
        gdma(scr.ap()[0:1, 0:H].rearrange("a h -> h a"), kmax_c[:, 0:1])
        w(GPS, "gs")
        gdma(crow[0:1, 0:H], scr.ap()[0:1, 0:H])
        w(VEC, "gs")
        VEC.tensor_tensor(crow[:], crow[:], qmax_r[:], op=ALU.mult)
        w(SCA, "dv")
        SCA.activation(crow[:], crow[:], AF.Sqrt)
        w(VEC, "ac")
        VEC.tensor_scalar(crow[:], crow[:], -1.0, None, op0=ALU.mult)
        VEC.tensor_copy(crow_b[:], crow[:])
        w(TEN, "dv")
        inc(TEN.matmul(psb2[:, :], ones_rb[:], crow_b[:], start=True, stop=True),
            "pe")
        w(VEC, "pe")
        VEC.tensor_copy(shifts[:], psb2[:, :])
        bar()

    ctx_ctx = contextlib.ExitStack()
    ctxT = ctx_ctx.enter_context(nc.sbuf_tensor("ctxT", [128, DC, QTOK], BF16))
    with contextlib.ExitStack() as st:
        kst = st.enter_context(nc.sbuf_tensor("a_k", [128, 2, S], BF16))
        vst = st.enter_context(nc.sbuf_tensor("a_v", [128, 2, NKC, 128], BF16))
        pT = st.enter_context(nc.sbuf_tensor("a_p", [128, 2, 2, 512], BF16))
        lrec = st.enter_context(nc.sbuf_tensor("a_lr", [1, 512], F32))
        lrecb = st.enter_context(nc.sbuf_tensor("a_lb16", [1, 512], BF16))
        lb = st.enter_context(nc.sbuf_tensor("a_lb", [128, 512], F32))
        psS = st.enter_context(nc.psum_tensor("a_s", [128, 2, 2, 512], F32))
        psC = st.enter_context(nc.psum_tensor("a_c", [128, 512], F32))
        psL = st.enter_context(nc.psum_tensor("a_l", [1, 512], F32))
        psB = st.enter_context(nc.psum_tensor("a_b", [128, 512], F32))
        pe_head_end = {}
        for h in range(H):
            par = h % 2
            if h >= 2:
                w(GPS, "pe", pe_head_end[h - 2])
            sl = ("ka", "kb")[par]
            gdma(kst[:, par, :], KT_g.ap()[128 * h:128 * h + 128, :], s=sl)
            gdma(vst[:, par, :, :],
                 V_g.ap()[:, 128 * h:128 * h + 128]
                 .rearrange("(kc p) d -> p kc d", p=128), s=sl)
            w(TEN, sl)
            w(SCA, sl)
            for qc in range(NQC):
                nkc = min(16 * (qc + 1), NKC)
                dense = nkc - 16
                ac_rounds = []
                ri = 0
                for kc0 in range(0, nkc, 2):
                    g2 = ri % 2
                    if ri >= 2:
                        w(TEN, "ac", ac_rounds[ri - 2])
                    for j, kc in enumerate((kc0, kc0 + 1)):
                        if kc >= nkc:
                            continue
                        qoff = 0 if kc < dense else 128 * ((kc - dense) // 4)
                        m = TEN.matmul(
                            psS[:, g2, j, qoff:],
                            kst[:, par, 128 * kc:128 * kc + 128],
                            qT[:, h, 512 * qc + qoff:512 * qc + 512],
                            start=True, stop=True)
                    inc(m, "pe")
                    w(SCA, "pe")
                    for j, kc in enumerate((kc0, kc0 + 1)):
                        if kc >= nkc:
                            continue
                        qoff = 0 if kc < dense else 128 * ((kc - dense) // 4)
                        SCA.activation(pT[:, g2, j, qoff:], psS[:, g2, j, qoff:],
                                       AF.Exp, bias=shifts[:, h:h + 1])
                    ac_rounds.append(cnt["ac"])
                    w(TEN, "ac")
                    w(VEC, "ac")
                    nm = None
                    for j, kc in enumerate((kc0, kc0 + 1)):
                        if kc >= nkc or kc < dense:
                            continue
                        wk = kc - dense
                        qoff = 128 * (wk // 4)
                        nm = VEC.tensor_tensor(pT[:, g2, j, qoff:qoff + 128],
                                               pT[:, g2, j, qoff:qoff + 128],
                                               masks[:, wk % 4, :], op=ALU.mult)
                    if nm is not None:
                        w(TEN, "dv")
                    for j, kc in enumerate((kc0, kc0 + 1)):
                        if kc >= nkc:
                            continue
                        qoff = 0 if kc < dense else 128 * ((kc - dense) // 4)
                        first, last = (kc == 0), (kc == nkc - 1)
                        TEN.matmul(psC[:, qoff:], vst[:, par, kc, :],
                                   pT[:, g2, j, qoff:], start=first, stop=last)
                        mL = TEN.matmul(psL[0:1, qoff:], ones_cb[:],
                                        pT[:, g2, j, qoff:], start=first, stop=last)
                    ri += 1
                inc(mL, "pe")
                w(VEC, "pe")
                VEC.reciprocal(lrec[:], psL[0:1, :])
                w(TEN, "dv")
                inc(TEN.matmul(psB[:, :], ones_rf[:], lrec[:],
                               start=True, stop=True), "pe")
                w(VEC, "pe")
                VEC.tensor_copy(lb[:], psB[:, :])
                VEC.tensor_tensor(ctxT[:, h, 512 * qc:512 * qc + 512],
                                      psC[:, :], lb[:], op=ALU.mult)
                w(TEN, "dv")
            pe_head_end[h] = cnt["pe"]
        bar()
    bar()

    if debug_taps:
        with nc.sbuf_tensor("dbg_t", [128, max(D, QTOK)], F32) as dbt:
            for tt in range(NQT):
                VEC.tensor_copy(dbt[:, 0:D], ffn_s[:, tt, :])
                w(SYN, "dv")
                hdma(dbg_ffn.ap()[128 * tt:128 * tt + 128, :], dbt[:, 0:D])
                w(VEC, "hs")
            for dc in range(DC):
                VEC.tensor_copy(dbt[:, 0:QTOK], ctxT[:, dc, :])
                w(SYN, "dv")
                hdma(dbg_ctx.ap()[128 * dc:128 * dc + 128, :], dbt[:, 0:QTOK])
                w(VEC, "hs")
                VEC.tensor_copy(dbt[:, 0:QTOK], qT[:, dc, :])
                w(SYN, "dv")
                hdma(dbg_qT.ap()[128 * dc:128 * dc + 128, :], dbt[:, 0:QTOK])
                w(VEC, "hs")
            w(SYN, "dv")
            hdma(dbg_shift.ap(), shifts[:])
            w(SYN, "hs")
            bar()

    with contextlib.ExitStack() as st:
        wo = st.enter_context(nc.sbuf_tensor("o_w", [128, 2, DC, OC], BF16))
        xrow = st.enter_context(nc.sbuf_tensor("o_x", [128, 2, OC], F32))
        acc1 = st.enter_context(nc.sbuf_tensor("o_a", [128, OC], F32))
        outt = st.enter_context(nc.sbuf_tensor("o_o", [128, 2, OC], F32))
        pso0 = st.enter_context(nc.psum_tensor("o_p0", [128, OC], F32))
        pso1 = st.enter_context(nc.psum_tensor("o_p1", [128, OC], F32))
        psol = [pso0, pso1]
        gdma(wo[:, 0], outW.ap()[:, 0:OC].rearrange("(c p) x -> p c x", p=128),
             s="ga")
        use_dv = []
        st_c = []
        pe_oc = []
        for oc in range(D // OC):
            if oc + 1 < D // OC:
                if oc >= 1:
                    w(GPS, "pe", pe_oc[oc - 1])
                gdma(wo[:, (oc + 1) % 2],
                     outW.ap()[:, OC * (oc + 1):OC * (oc + 1) + OC]
                     .rearrange("(c p) x -> p c x", p=128),
                     s=("ga", "gb")[(oc + 1) % 2])
            w(TEN, ("ga", "gb")[oc % 2])
            for tt in range(NQT):
                g = oc * NQT + tt
                p = psol[g % 2]
                lsl = ("ha", "hb")[g % 2]
                ssl = ("sa", "sb")[g % 2]
                if g >= 2:
                    w(SYN, "dv", use_dv[g - 2])
                ld = hdma(xrow[:, g % 2, :],
                          x_q.ap()[128 * tt:128 * tt + 128,
                                   OC * oc:OC * oc + OC], s=lsl)
                if g >= 2:
                    w(TEN, "dv", use_dv[g - 2])
                for dc in range(DC):
                    m = TEN.matmul(p[:, :], ctxT[:, dc, 128 * tt:128 * tt + 128],
                                   wo[:, oc % 2, dc],
                                   start=(dc == 0), stop=(dc == DC - 1))
                inc(m, "pe")
                w(VEC, "pe")
                w(VEC, lsl, ld)
                if g >= 2:
                    w(VEC, ssl, st_c[g - 2])
                VEC.tensor_tensor(acc1[:], p[:, :],
                                  ffn_s[:, tt, OC * oc:OC * oc + OC],
                                  op=ALU.add)
                VEC.tensor_tensor(outt[:, g % 2, :], acc1[:],
                                  xrow[:, g % 2, :], op=ALU.add)
                use_dv.append(cnt["dv"])
                w(SYN, "dv")
                st_c.append(
                    hdma(out_d.ap()[128 * tt:128 * tt + 128,
                                    OC * oc:OC * oc + OC],
                         outt[:, g % 2, :], s=ssl))
            pe_oc.append(cnt["pe"])
        w(SYN, "sa")
        w(SYN, "sb")
        w(GPS, "sa")
        w(GPS, "sb")
    bar()
    ctx_ctx.close()
    es.close()
    nc.finalize()
    return nc


_NC_CACHE = {}


def kernel(L0_attn_x, L0_attn_Wqkv, L0_attn_q_ln_scale, L0_attn_k_ln_scale,
           L0_attn_out_W, L0_ffn_W, L0_ffn_bias, L0_ffn_rw1, L0_ffn_rb1,
           L0_ffn_rw2, L0_ffn_rb2, L0_ffn_A, L0_ffn_B, L0_ffn_gates):
    cfg = REAL
    if "nc" not in _NC_CACHE:
        _NC_CACHE["nc"] = build_nc(cfg)
    nc = _NC_CACHE["nc"]
    in_maps = prep_inputs(cfg, L0_attn_x, L0_attn_Wqkv, L0_attn_q_ln_scale,
                          L0_attn_k_ln_scale, L0_attn_out_W, L0_ffn_W,
                          L0_ffn_bias, L0_ffn_rw1, L0_ffn_rb1, L0_ffn_rw2,
                          L0_ffn_rb2, L0_ffn_A, L0_ffn_B, L0_ffn_gates)
    res = run_bass_kernel_spmd(nc, in_maps, core_ids=list(range(NCORES)))
    out = np.empty((cfg.B, cfg.S, cfg.D), np.float32)
    for c in range(NCORES):
        b, cp = c // 4, c % 4
        out[b, cp::4, :] = res.results[c]["out"]
    return out


# version 04
# speedup vs baseline: 1.0506x; 1.0506x over previous
import sys
sys.path.insert(0, "/opt/trn_rl_repo")
import contextlib
import numpy as np
import concourse.bass as bass
import concourse.mybir as mybir
from concourse.bass_utils import run_bass_kernel_spmd

F32 = mybir.dt.float32
BF16 = mybir.dt.bfloat16
I32 = mybir.dt.int32
AF = mybir.ActivationFunctionType
ALU = mybir.AluOpType
AX = mybir.AxisListType

NCORES = 8
MAGIC = 1337


class Cfg:
    def __init__(self, B=2, S=4096, D=2048, H=16):
        self.B, self.S, self.D, self.H = B, S, D, H
        self.HD = D // H
        assert self.HD == 128 and S % 2048 == 0
        self.OC = min(512, D)
        self.DC = D // 128
        self.QTOK = S // 4
        self.KTOK = S // 2
        self.NQC = self.QTOK // 512
        self.NKC = S // 128
        self.NQT = self.QTOK // 128
        self.RANK, self.NE, self.RH = 16, 16, 64
        self.ER = self.RANK * self.NE


REAL = Cfg()


def make_masks(cprime):
    import ml_dtypes
    kk = np.arange(128)[:, None]
    jj = np.arange(128)[None, :]
    m = np.stack([(128 * w4 + kk <= 4 * jj + cprime) for w4 in range(4)])
    return np.ascontiguousarray(m.astype(ml_dtypes.bfloat16))


def prep_inputs(cfg, x, Wqkv, q_ln_scale, k_ln_scale, out_W,
                W, bias, rw1, rb1, rw2, rb2, A, B_, gates):
    B, S, D = cfg.B, cfg.S, cfg.D
    x = np.asarray(x)
    A = np.asarray(A); B_ = np.asarray(B_)
    AT = np.ascontiguousarray(A.transpose(2, 0, 1).reshape(D, cfg.ER))
    BT = np.ascontiguousarray(B_.transpose(0, 2, 1).reshape(cfg.ER, D))
    WT = np.ascontiguousarray(np.asarray(W).T)
    rw1T = np.ascontiguousarray(np.asarray(rw1).T)
    rw2T = np.ascontiguousarray(np.asarray(rw2).T[:, :cfg.NE])
    import ml_dtypes
    ident = np.eye(128, dtype=ml_dtypes.bfloat16)
    in_maps = []
    for c in range(NCORES):
        b, cp = c // 4, c % 4
        kvh = cp % 2
        qidx = np.arange(cp, S, 4)
        kvs = slice(kvh * (S // 2), (kvh + 1) * (S // 2))
        xb = x[b]
        in_maps.append({
            "x_qT": np.ascontiguousarray(xb[qidx].T),
            "x_kvT": np.ascontiguousarray(xb[kvs].T),
            "x_q": np.ascontiguousarray(xb[qidx]),
            "Wqkv": np.asarray(Wqkv),
            "outW": np.asarray(out_W),
            "WT": WT,
            "rw1T": rw1T,
            "rw2T": rw2T,
            "AT": AT,
            "BT": BT,
            "qls": np.asarray(q_ln_scale).reshape(1, D),
            "kls": np.asarray(k_ln_scale).reshape(1, D),
            "fbias": np.asarray(bias).reshape(1, D),
            "rb1": np.asarray(rb1).reshape(1, cfg.RH),
            "rb2": np.ascontiguousarray(np.asarray(rb2)[:cfg.NE].reshape(1, cfg.NE)),
            "gates": np.asarray(gates).reshape(1, cfg.NE),
            "masks": make_masks(cp),
            "ident": ident,
            "kvhalf": np.array([[kvh]], np.int32),
        })
    return in_maps


def build_nc(cfg, detect_races=False, debug_taps=False):
    nc = bass.Bass("TRN2", target_bir_lowering=False, num_devices=NCORES,
                   detect_race_conditions=detect_races,
                   dynamic_dma_scratch_size=16384)
    D, DC, H, S = cfg.D, cfg.DC, cfg.H, cfg.S
    QTOK, KTOK, NQC, NKC, NQT = cfg.QTOK, cfg.KTOK, cfg.NQC, cfg.NKC, cfg.NQT
    ER, NE, RH, RANK = cfg.ER, cfg.NE, cfg.RH, cfg.RANK
    OC = cfg.OC

    es = contextlib.ExitStack()
    sems, cnt = {}, {}

    def sem(n):
        sems[n] = es.enter_context(nc.semaphore(n))
        cnt[n] = 0

    for n in ("hs", "gs", "pe", "dv", "ac", "po", "ga", "gb", "ha", "hb", "sa", "sb", "ka", "kb", "va", "vb", "pb"):
        sem(n)

    def inc(instr, n, v=1):
        instr.then_inc(sems[n], v)
        cnt[n] += v
        return cnt[n]

    def hdma(dst, src, s="hs"):
        return inc(nc.sync.dma_start(dst, src), s, 16)

    def gdma(dst, src, cond=None, s="gs"):
        if cond is None:
            return inc(nc.gpsimd.dma_start(dst, src), s, 16)
        return inc(nc.gpsimd.dma_start(dst, src, cond=cond), s, 16)

    def w(engine, n, v=None):
        engine.wait_ge(sems[n], cnt[n] if v is None else v)

    def gdma_nc(dst, src, s="gs"):
        nc.gpsimd.dma_start(dst, src).then_inc(sems[s], 16)

    bar = nc.all_engine_barrier

    _WRAP = {"tensor_tensor", "tensor_scalar", "tensor_copy", "reduce_max",
             "reduce_sum", "reciprocal", "memset", "tensor_single_scalar",
             "activation", "select", "tensor_add", "tensor_mul", "tensor_sub"}

    class _Proxy:
        def __init__(self, eng, s):
            self._e, self._s = eng, s

        def __getattr__(self, name):
            attr = getattr(self._e, name)
            if name in _WRAP:
                def f(*a, **k):
                    self._e.wait_ge(sems[self._s], cnt[self._s])
                    r = attr(*a, **k)
                    r.then_inc(sems[self._s], 1)
                    cnt[self._s] += 1
                    return r
                return f
            return attr

    TEN, GPS, SYN = nc.tensor, nc.gpsimd, nc.sync
    VEC = _Proxy(nc.vector, "dv")
    SCA = _Proxy(nc.scalar, "ac")

    dt = lambda n, s, d=F32, kind="ExternalInput", **kw: \
        nc.dram_tensor(n, s, d, kind=kind, **kw)

    x_qT = dt("x_qT", [D, QTOK])
    x_kvT = dt("x_kvT", [D, KTOK])
    x_q = dt("x_q", [QTOK, D])
    Wqkv = dt("Wqkv", [D, 3 * D])
    outW = dt("outW", [D, D])
    WT_d = dt("WT", [D, D])
    rw1T = dt("rw1T", [D, RH])
    rw2T = dt("rw2T", [RH, NE])
    AT_d = dt("AT", [D, ER])
    BT_d = dt("BT", [ER, D])
    qls_d = dt("qls", [1, D])
    kls_d = dt("kls", [1, D])
    fbias = dt("fbias", [1, D])
    rb1_d = dt("rb1", [1, RH])
    rb2_d = dt("rb2", [1, NE])
    gates_d = dt("gates", [1, NE])
    masks_d = dt("masks", [4, 128, 128], BF16)
    ident_d = dt("ident", [128, 128], BF16)
    kvhalf_d = dt("kvhalf", [1, 1], I32)
    out_d = dt("out", [QTOK, D], kind="ExternalOutput")
    if debug_taps:
        dbg_ffn = dt("dbg_ffn", [NQT * 128, D], kind="ExternalOutput")
        dbg_ctx = dt("dbg_ctx", [D, QTOK], kind="ExternalOutput")
        dbg_qT = dt("dbg_qT", [D, QTOK], kind="ExternalOutput")
        dbg_shift = dt("dbg_shift", [128, H], kind="ExternalOutput")

    KT_g = dt("KT_g", [D, S], BF16, kind="Internal", addr_space="Shared")
    V_g = dt("V_g", [S, D], BF16, kind="Internal", addr_space="Shared")
    KN_g = dt("KN_g", [H, S], F32, kind="Internal", addr_space="Shared")
    FLG = dt("FLG", [2, 1], I32, kind="Internal", addr_space="Shared")
    scr = dt("scr", [1, 64], F32, kind="Internal")
    KT_l = dt("KT_l", [D, KTOK], BF16, kind="Internal")
    V_l = dt("V_l", [KTOK, D], BF16, kind="Internal")
    KN_l = dt("KN_l", [H, KTOK], F32, kind="Internal")
    xnq_d = dt("xnq_d", [D, QTOK], BF16, kind="Internal")
    xne_d = dt("xne_d", [D, QTOK], BF16, kind="Internal")

    qT = es.enter_context(nc.sbuf_tensor("qT", [128, DC, QTOK], BF16))
    ffn_s = es.enter_context(nc.sbuf_tensor("ffn_s", [128, NQT, D], BF16))
    ones_cb = es.enter_context(nc.sbuf_tensor("ones_cb", [128, 1], BF16))
    ones_rb = es.enter_context(nc.sbuf_tensor("ones_rb", [1, 128], BF16))
    ones_cf = es.enter_context(nc.sbuf_tensor("ones_cf", [128, 1], F32))
    ident = es.enter_context(nc.sbuf_tensor("ident_s", [128, 128], BF16))
    masks = es.enter_context(nc.sbuf_tensor("masks_s", [128, 4, 128], BF16))
    kvh_s = es.enter_context(nc.sbuf_tensor("kvh_s", [1, 1], I32))
    brow = es.enter_context(nc.sbuf_tensor("brow", [1, NE], BF16))
    rb1_s = es.enter_context(nc.sbuf_tensor("rb1_s", [1, RH], BF16))
    shifts = es.enter_context(nc.sbuf_tensor("shifts", [128, H], F32))
    qmax_r = es.enter_context(nc.sbuf_tensor("qmax_r", [1, H], F32))
    flagv = es.enter_context(nc.sbuf_tensor("flagv", [1, 1], I32))
    magic_s = es.enter_context(nc.sbuf_tensor("magic_s", [1, 1], I32))
    zero_s = es.enter_context(nc.sbuf_tensor("zero_s", [1, 1], I32))
    srow2 = es.enter_context(nc.sbuf_tensor("srow2", [1, 2 * NE], F32))
    eps5 = es.enter_context(nc.sbuf_tensor("eps5", [1, 1], F32))
    ones_rf = es.enter_context(nc.sbuf_tensor("ones_rf", [1, 128], F32))
    identF = es.enter_context(nc.sbuf_tensor("identF", [128, 128], F32))
    brow_f = es.enter_context(nc.sbuf_tensor("brow_f", [1, NE], F32))
    eps6 = es.enter_context(nc.sbuf_tensor("eps6", [1, 1], F32))

    gdma(kvh_s[:], kvhalf_d.ap())
    gdma(masks[:], masks_d.ap().rearrange("w p q -> p w q"))
    gdma(ident[:], ident_d.ap())
    gdma(srow2[0:1, 0:NE], rb2_d.ap())
    gdma(srow2[0:1, NE:2 * NE], gates_d.ap())
    gdma(rb1_s[:], rb1_d.ap())
    w(GPS, "gs")
    GPS.memset(ones_cb[:], 1.0)
    GPS.memset(ones_rb[:], 1.0)
    GPS.memset(ones_cf[:], 1.0)
    GPS.memset(ones_rf[:], 1.0)
    GPS.memset(magic_s[:], MAGIC)
    GPS.memset(eps5[:], 1e-5)
    GPS.memset(eps6[:], 1e-6)
    GPS.memset(zero_s[:], 0)
    hv = GPS.value_load(kvh_s[0:1, 0:1])
    bar()
    with GPS.If(hv == 0):
        gdma(FLG.ap()[0:1, :], zero_s[:])
    with GPS.Else():
        gdma_nc(FLG.ap()[1:2, :], zero_s[:])
    w(GPS, "gs")
    VEC.tensor_tensor(brow_f[:], srow2[0:1, 0:NE], srow2[0:1, NE:2 * NE],
                      op=ALU.add)
    VEC.tensor_copy(identF[:], ident[:])
    bar()

    def ln_fm(xsrc, t0, T, dst, eps_t, tag, dst_err=None):
        NT = T // 512
        with contextlib.ExitStack() as st:
            xa = st.enter_context(nc.sbuf_tensor(f"lnx_{tag}", [128, 2, T], F32))
            xsq = st.enter_context(nc.sbuf_tensor(f"lnq_{tag}", [128, T], F32))
            mrow = st.enter_context(nc.sbuf_tensor(f"lnm_{tag}", [1, T], F32))
            vrow = st.enter_context(nc.sbuf_tensor(f"lnv_{tag}", [1, T], F32))
            trow = st.enter_context(nc.sbuf_tensor(f"lnt_{tag}", [1, T], F32))
            mrow_b = st.enter_context(nc.sbuf_tensor(f"lnmb_{tag}", [1, T], BF16))
            rrow_b = st.enter_context(nc.sbuf_tensor(f"lnrb_{tag}", [1, T], BF16))
            mb = st.enter_context(nc.sbuf_tensor(f"lnM_{tag}", [128, T], BF16))
            rb = st.enter_context(nc.sbuf_tensor(f"lnR_{tag}", [128, T], BF16))
            ps_s = st.enter_context(nc.psum_tensor(f"lns_{tag}", [1, T], F32))
            ps_q = st.enter_context(nc.psum_tensor(f"lnp_{tag}", [1, T], F32))
            ps_b = st.enter_context(nc.psum_tensor(f"lnw_{tag}", [128, 512], F32))
            pe0 = cnt["pe"]
            for dc in range(DC):
                if dc >= 2:
                    w(SYN, "pe", pe0 + dc - 1)
                sl = ("ha", "hb")[dc % 2]
                hdma(xa[:, dc % 2, :],
                     xsrc.ap()[:, t0:t0 + T].rearrange("(c p) t -> c p t", p=128)[dc],
                     s=sl)
                w(TEN, sl)
                w(SCA, sl)
                for t in range(NT):
                    TEN.matmul(ps_s[0:1, 512 * t:512 * t + 512], ones_cf[:],
                               xa[:, dc % 2, 512 * t:512 * t + 512],
                               start=(dc == 0), stop=(dc == DC - 1))
                if dc > 0:
                    w(SCA, "pe", pe0 + dc)
                SCA.activation(xsq[:], xa[:, dc % 2, :], AF.Square)
                w(TEN, "ac")
                for t in range(NT):
                    m = TEN.matmul(ps_q[0:1, 512 * t:512 * t + 512], ones_cf[:],
                                   xsq[:, 512 * t:512 * t + 512],
                                   start=(dc == 0), stop=(dc == DC - 1))
                    if t == NT - 1:
                        inc(m, "pe")
            bar()
            VEC.tensor_scalar(mrow[:], ps_s[0:1, :], 1.0 / D, None, op0=ALU.mult)
            VEC.tensor_scalar(vrow[:], ps_q[0:1, :], 1.0 / D, None, op0=ALU.mult)
            VEC.tensor_tensor(trow[:], mrow[:], mrow[:], op=ALU.mult)
            VEC.tensor_tensor(vrow[:], vrow[:], trow[:], op=ALU.subtract)
            w(SCA, "dv")
            SCA.activation(vrow[:], vrow[:], AF.Sqrt, bias=eps_t[0:1, :])
            w(VEC, "ac")
            VEC.reciprocal(vrow[:], vrow[:])
            VEC.tensor_copy(mrow_b[:], mrow[:])
            VEC.tensor_copy(rrow_b[:], vrow[:])
            w(TEN, "dv")
            for t in range(NT):
                for rsrc, dstb in ((mrow_b, mb), (rrow_b, rb)):
                    inc(TEN.matmul(ps_b[:, :], ones_rb[:],
                                   rsrc[0:1, 512 * t:512 * t + 512],
                                   start=True, stop=True), "pe")
                    w(VEC, "pe")
                    VEC.tensor_copy(dstb[:, 512 * t:512 * t + 512], ps_b[:, :])
                    w(TEN, "dv")
            bar()
            dv0 = cnt["dv"]
            opd = 2 if dst_err is None else 4
            for dc in range(DC):
                if dc >= 2:
                    w(SYN, "dv", dv0 + opd * (dc - 1))
                sl = ("ha", "hb")[dc % 2]
                hdma(xa[:, dc % 2, :],
                     xsrc.ap()[:, t0:t0 + T].rearrange("(c p) t -> c p t", p=128)[dc],
                     s=sl)
                w(VEC, sl)
                VEC.tensor_tensor(xsq[:], xa[:, dc % 2, :], mb[:], op=ALU.subtract)
                if dst_err is None:
                    VEC.tensor_tensor(dst[:, dc, :], xsq[:], rb[:], op=ALU.mult)
                else:
                    VEC.tensor_tensor(xsq[:], xsq[:], rb[:], op=ALU.mult)
                    VEC.tensor_copy(dst[:, dc, :], xsq[:])
                    VEC.tensor_tensor(dst_err[:, dc, :], xsq[:], dst[:, dc, :],
                                      op=ALU.subtract)
            bar()

    def proj_ln(cb, xn_src, T, scale_tile, eps_t, tag, sink, nrm_sink=None,
                rstd_scale=None):
        NT = T // 512
        with contextlib.ExitStack() as st:
            wb = st.enter_context(nc.sbuf_tensor(f"pw_{tag}", [128, 2, DC, 128], BF16))
            acc = st.enter_context(nc.sbuf_tensor(f"pa_{tag}", [128, DC, T], BF16))
            outc = st.enter_context(nc.sbuf_tensor(f"po_{tag}", [128, 2, T], BF16))
            sq = st.enter_context(nc.sbuf_tensor(f"pq_{tag}", [128, 2, T], BF16))
            mrow = st.enter_context(nc.sbuf_tensor(f"pm_{tag}", [1, T], F32))
            vrow = st.enter_context(nc.sbuf_tensor(f"pv_{tag}", [1, T], F32))
            mrow_b = st.enter_context(nc.sbuf_tensor(f"pmb_{tag}", [1, T], BF16))
            rrow_b = st.enter_context(nc.sbuf_tensor(f"prb_{tag}", [1, T], BF16))
            mb = st.enter_context(nc.sbuf_tensor(f"pM_{tag}", [128, T], BF16))
            rsb = st.enter_context(nc.sbuf_tensor(f"pR_{tag}", [128, 2, T], BF16))
            nrow = st.enter_context(nc.sbuf_tensor(f"pn_{tag}", [1, T], F32))
            ps0 = st.enter_context(nc.psum_tensor(f"p0_{tag}", [128, 512], F32))
            ps1 = st.enter_context(nc.psum_tensor(f"p1_{tag}", [128, 512], F32))
            ps_s = st.enter_context(nc.psum_tensor(f"p2_{tag}", [1, T], F32))
            ps_q = st.enter_context(nc.psum_tensor(f"p3_{tag}", [1, T], F32))
            ps_b = st.enter_context(nc.psum_tensor(f"p4_{tag}", [128, 512], F32))
            psl = [ps0, ps1]
            pe_cc = []
            stats_pe = {}
            square_ac = {}
            evac_dvs = {}
            gdma(wb[:, 0], Wqkv.ap()[:, cb:cb + 128]
                 .rearrange("(c p) x -> p c x", p=128), s="ga")
            dv0, pe0 = cnt["dv"], cnt["pe"]
            gtile = 0

            def emit_square(j):
                w(SCA, "dv", evac_dvs[j])
                if j >= 2:
                    w(SCA, "pe", stats_pe[j - 2])
                SCA.activation(sq[:, j % 2, :], acc[:, j, :], AF.Square)
                square_ac[j] = cnt["ac"]

            def emit_stats(j):
                w(TEN, "ac", square_ac[j])
                for t in range(NT):
                    TEN.matmul(ps_s[0:1, 512 * t:512 * t + 512], ones_cb[:],
                               acc[:, j, 512 * t:512 * t + 512],
                               start=(j == 0), stop=(j == DC - 1))
                    m = TEN.matmul(ps_q[0:1, 512 * t:512 * t + 512], ones_cb[:],
                                   sq[:, j % 2, 512 * t:512 * t + 512],
                                   start=(j == 0), stop=(j == DC - 1))
                inc(m, "pe")
                stats_pe[j] = cnt["pe"]

            for cc in range(DC):
                if cc + 1 < DC:
                    if cc >= 1:
                        w(GPS, "pe", pe_cc[cc - 1])
                    gdma(wb[:, (cc + 1) % 2],
                         Wqkv.ap()[:, cb + 128 * (cc + 1):cb + 128 * (cc + 1) + 128]
                         .rearrange("(c p) x -> p c x", p=128),
                         s=("ga", "gb")[(cc + 1) % 2])
                w(TEN, ("ga", "gb")[cc % 2])
                for t in range(NT):
                    p = psl[gtile % 2]
                    if gtile >= 2:
                        w(TEN, "dv", dv0 + gtile - 1)
                    for dc in range(DC):
                        m = TEN.matmul(p[:, :], wb[:, cc % 2, dc],
                                       xn_src[:, dc, 512 * t:512 * t + 512],
                                       start=(dc == 0), stop=(dc == DC - 1))
                    inc(m, "pe")
                    w(VEC, "pe")
                    VEC.tensor_copy(acc[:, cc, 512 * t:512 * t + 512], p[:, :])
                    gtile += 1
                evac_dvs[cc] = cnt["dv"]
                pe_cc.append(cnt["pe"])
                emit_square(cc)
                if cc >= 1:
                    emit_stats(cc - 1)
            emit_stats(DC - 1)
            bar()
            VEC.tensor_scalar(mrow[:], ps_s[0:1, :], 1.0 / D, None, op0=ALU.mult)
            VEC.tensor_scalar(vrow[:], ps_q[0:1, :], 1.0 / D, None, op0=ALU.mult)
            VEC.tensor_tensor(nrow[:], mrow[:], mrow[:], op=ALU.mult)
            VEC.tensor_tensor(vrow[:], vrow[:], nrow[:], op=ALU.subtract)
            w(SCA, "dv")
            SCA.activation(vrow[:], vrow[:], AF.Sqrt, bias=eps_t[0:1, :])
            w(VEC, "ac")
            VEC.reciprocal(vrow[:], vrow[:])
            if rstd_scale is not None:
                VEC.tensor_scalar(vrow[:], vrow[:], float(rstd_scale), None,
                                  op0=ALU.mult)
            VEC.tensor_copy(mrow_b[:], mrow[:])
            VEC.tensor_copy(rrow_b[:], vrow[:])
            w(TEN, "dv")
            for t in range(NT):
                inc(TEN.matmul(ps_b[:, :], ones_rb[:],
                               mrow_b[0:1, 512 * t:512 * t + 512],
                               start=True, stop=True), "pe")
                w(VEC, "pe")
                VEC.tensor_copy(mb[:, 512 * t:512 * t + 512], ps_b[:, :])
                w(TEN, "dv")
            bar()
            for dc in range(DC):
                pp = dc % 2
                for t in range(NT):
                    inc(TEN.matmul(ps_b[:, :],
                                   scale_tile[0:1, 128 * dc:128 * dc + 128],
                                   rrow_b[0:1, 512 * t:512 * t + 512],
                                   start=True, stop=True), "pe")
                    w(VEC, "pe")
                    VEC.tensor_copy(rsb[:, pp, 512 * t:512 * t + 512],
                                        ps_b[:, :])
                    w(TEN, "dv")
                VEC.tensor_tensor(sq[:, 0, :], acc[:, dc, :], mb[:], op=ALU.subtract)
                VEC.tensor_tensor(outc[:, pp, :], sq[:, 0, :], rsb[:, pp, :],
                                      op=ALU.mult)
                sink(dc, outc[:, pp, :])
                if nrm_sink is not None:
                    w(SCA, "dv")
                    SCA.activation(sq[:, 1, :], outc[:, pp, :], AF.Square)
                    w(TEN, "ac")
                    for t in range(NT):
                        m = TEN.matmul(ps_s[0:1, 512 * t:512 * t + 512], ones_cb[:],
                                       sq[:, 1, 512 * t:512 * t + 512],
                                       start=True, stop=True)
                    inc(m, "pe")
                    w(VEC, "pe")
                    VEC.tensor_copy(nrow[0:1, :], ps_s[0:1, :])
                    nrm_sink(dc, nrow[0:1, :])
            bar()

    xnq_ctx = contextlib.ExitStack()
    xnT_q = xnq_ctx.enter_context(nc.sbuf_tensor("xnT_q", [128, DC, QTOK], BF16))
    qls_s = xnq_ctx.enter_context(nc.sbuf_tensor("qls_s", [1, D], BF16))
    gdma(qls_s[:], qls_d.ap())
    w(GPS, "gs")
    xn_err = xnq_ctx.enter_context(nc.sbuf_tensor("xn_err", [128, DC, QTOK], BF16))
    ln_fm(x_qT, 0, QTOK, xnT_q, eps5, "q", dst_err=xn_err)

    def q_sink(dc, tile):
        VEC.tensor_copy(qT[:, dc, :], tile)

    def q_nrm(dc, row):
        VEC.reduce_max(qmax_r[0:1, dc:dc + 1], row, axis=AX.X)

    proj_ln(0, xnT_q, QTOK, qls_s, eps6, "q", q_sink, q_nrm,
            rstd_scale=1.0 / float(np.sqrt(128.0)))
    bar()

    gdma(xnq_d.ap().rearrange("(c p) t -> p c t", p=128), xnT_q[:])
    gdma(xne_d.ap().rearrange("(c p) t -> p c t", p=128), xn_err[:])
    w(GPS, "gs")
    bar()
    xnq_ctx.close()

    TKV = KTOK // 2
    with contextlib.ExitStack() as kv_ctx:
        xnT_kv = kv_ctx.enter_context(
            nc.sbuf_tensor("xnT_kv", [128, DC, TKV], BF16))
        kls_s = kv_ctx.enter_context(nc.sbuf_tensor("kls_s", [1, D], BF16))
        gdma(kls_s[:], kls_d.ap())
        w(GPS, "gs")
        VOC = min(256, OC)
        vw = kv_ctx.enter_context(nc.sbuf_tensor("vw", [128, 2, DC, VOC], BF16))
        vout = kv_ctx.enter_context(nc.sbuf_tensor("vout", [128, 2, VOC], BF16))
        for sh in range(2):
            t0 = sh * TKV
            ln_fm(x_kvT, t0, TKV, xnT_kv, eps5, f"kv{sh}")

            def k_sink(dc, tile, _t0=t0):
                w(GPS, "dv")
                gdma(KT_l.ap()[128 * dc:128 * dc + 128, _t0:_t0 + TKV], tile)
                w(VEC, "gs")

            def k_nrm(dc, row, _t0=t0):
                w(GPS, "dv")
                gdma(KN_l.ap()[dc:dc + 1, _t0:_t0 + TKV], row)
                w(VEC, "gs")

            proj_ln(D, xnT_kv, TKV, kls_s, eps6, f"k{sh}", k_sink, k_nrm)
            bar()
            NKT = TKV // 128
            with contextlib.ExitStack() as st:
                psv0 = st.enter_context(nc.psum_tensor(f"v0_{sh}", [128, VOC], F32))
                psv1 = st.enter_context(nc.psum_tensor(f"v1_{sh}", [128, VOC], F32))
                psvl = [psv0, psv1]
                gdma(vw[:, 0], Wqkv.ap()[:, 2 * D:2 * D + VOC]
                     .rearrange("(c p) x -> p c x", p=128), s="ga")
                evac_dv = []
                store_gs = []
                pe_cc = []
                for cc in range(D // VOC):
                    if cc + 1 < D // VOC:
                        if cc >= 1:
                            w(GPS, "pe", pe_cc[cc - 1])
                        gdma(vw[:, (cc + 1) % 2],
                             Wqkv.ap()[:, 2 * D + VOC * (cc + 1):
                                       2 * D + VOC * (cc + 1) + VOC]
                             .rearrange("(c p) x -> p c x", p=128),
                             s=("ga", "gb")[(cc + 1) % 2])
                    w(TEN, ("ga", "gb")[cc % 2])
                    for tt in range(NKT):
                        g = cc * NKT + tt
                        p = psvl[g % 2]
                        if g >= 2:
                            w(TEN, "dv", evac_dv[g - 2])
                        for dc in range(DC):
                            m = TEN.matmul(p[:, :],
                                           xnT_kv[:, dc, 128 * tt:128 * tt + 128],
                                           vw[:, cc % 2, dc],
                                           start=(dc == 0), stop=(dc == DC - 1))
                        inc(m, "pe")
                        w(VEC, "pe")
                        if g >= 2:
                            w(VEC, ("va", "vb")[g % 2], store_gs[g - 2])
                        VEC.tensor_copy(vout[:, g % 2, :], p[:, :])
                        evac_dv.append(cnt["dv"])
                        w(GPS, "dv")
                        gdma(V_l.ap()[t0 + 128 * tt:t0 + 128 * tt + 128,
                                      VOC * cc:VOC * cc + VOC],
                             vout[:, g % 2, :], s=("va", "vb")[g % 2])
                        store_gs.append(cnt[("va", "vb")[g % 2]])
                    pe_cc.append(cnt["pe"])
                    bar()
                w(GPS, "va")
                w(GPS, "vb")
                bar()
    bar()
    w(GPS, "gs")
    w(GPS, "ga")
    w(GPS, "gb")
    w(GPS, "va")
    w(GPS, "vb")
    PR = min(512, D)
    PV = min(1024, KTOK)
    hv2 = nc.sync.value_load(kvh_s[0:1, 0:1])

    def publish(hh, emit):
        for rr in range(0, D, PR):
            emit(KT_g.ap()[rr:rr + PR, hh * KTOK:(hh + 1) * KTOK],
                 KT_l.ap()[rr:rr + PR, :])
        for rr in range(0, KTOK, PV):
            emit(V_g.ap()[hh * KTOK + rr: hh * KTOK + rr + PV, :],
                 V_l.ap()[rr:rr + PV, :])
        emit(KN_g.ap()[:, hh * KTOK:(hh + 1) * KTOK], KN_l.ap())

    w(SYN, "gs")
    w(SYN, "va")
    w(SYN, "vb")
    with nc.sync.If(hv2 == 0):
        publish(0, lambda d_, s_: hdma(d_, s_, s="pb"))
    with nc.sync.Else():
        publish(1, lambda d_, s_: nc.sync.dma_start(d_, s_).then_inc(sems["pb"], 16))
    w(SYN, "pb")
    with nc.sync.If(hv2 == 0):
        hdma(FLG.ap()[0:1, :], magic_s[:], s="pb")
    with nc.sync.Else():
        nc.sync.dma_start(FLG.ap()[1:2, :], magic_s[:]).then_inc(sems["pb"], 16)

    with contextlib.ExitStack() as st:
        wxaT_a = st.enter_context(nc.sbuf_tensor("wxaT_a", [128, NQT, 2, 128], BF16))
        xnT_q = st.enter_context(nc.sbuf_tensor("xnq2", [128, DC, QTOK], BF16))
        xn_err = st.enter_context(nc.sbuf_tensor("xne2", [128, DC, QTOK], BF16))
        gdma(xnT_q[:], xnq_d.ap().rearrange("(c p) t -> p c t", p=128))
        gdma(xn_err[:], xne_d.ap().rearrange("(c p) t -> p c t", p=128))
        r1f = st.enter_context(nc.sbuf_tensor("f_r1f", [128, DC, RH], F32))
        r1b = st.enter_context(nc.sbuf_tensor("f_r1", [128, DC, RH], BF16))
        r1l = st.enter_context(nc.sbuf_tensor("f_r1l", [128, DC, RH], BF16))
        r2f = st.enter_context(nc.sbuf_tensor("f_r2", [RH, NE], F32))
        rb1f = st.enter_context(nc.sbuf_tensor("f_rb1f", [1, RH], F32))
        rb1lo = st.enter_context(nc.sbuf_tensor("f_rb1lo", [1, RH], BF16))
        atb = st.enter_context(nc.sbuf_tensor("f_at", [128, DC, ER], BF16))
        btb = st.enter_context(nc.sbuf_tensor("f_bt", [128, 2, D], BF16))
        bias_b = st.enter_context(nc.sbuf_tensor("f_bias", [1, D], BF16))
        hrow = st.enter_context(nc.sbuf_tensor("f_h", [128, RH], F32))
        hrowT = st.enter_context(nc.sbuf_tensor("f_hT", [RH, 128], F32))
        logi = st.enter_context(nc.sbuf_tensor("f_lg", [128, NE], F32))
        m1 = st.enter_context(nc.sbuf_tensor("f_m1", [128, 1], F32))
        m1n = st.enter_context(nc.sbuf_tensor("f_m1n", [128, 1], F32))
        m2 = st.enter_context(nc.sbuf_tensor("f_m2", [128, 1], F32))
        eq = st.enter_context(nc.sbuf_tensor("f_eq", [128, NE], F32))
        ee = st.enter_context(nc.sbuf_tensor("f_ee", [128, NE], F32))
        den = st.enter_context(nc.sbuf_tensor("f_den", [128, 1], F32))
        coeff = st.enter_context(nc.sbuf_tensor("f_cf", [128, NE], F32))
        crep = st.enter_context(nc.sbuf_tensor("f_cr", [128, ER], BF16))
        xa_s = st.enter_context(nc.sbuf_tensor("f_xa", [128, ER], F32))
        wxa = st.enter_context(nc.sbuf_tensor("f_wxa", [128, ER], BF16))
        FOC = min(256, OC)
        wfb = st.enter_context(nc.sbuf_tensor("f_w", [128, 2, DC, FOC], BF16))
        psh = st.enter_context(nc.psum_tensor("f_ph", [128, RH], F32))
        psg = st.enter_context(nc.psum_tensor("f_pg", [128, NE], F32))
        psx = st.enter_context(nc.psum_tensor("f_px", [128, ER], F32))
        pst = st.enter_context(nc.psum_tensor("f_pt", [128, 2, 128], BF16))
        psh2 = st.enter_context(nc.psum_tensor("f_ph2", [RH, 128], F32))
        psf0 = st.enter_context(nc.psum_tensor("f_p0", [128, 256 if OC >= 256 else OC], F32))
        psf1 = st.enter_context(nc.psum_tensor("f_p1", [128, 256 if OC >= 256 else OC], F32))
        psfl = [psf0, psf1]
        gdma(r1f[:], rw1T.ap().rearrange("(c p) r -> p c r", p=128))
        gdma(r2f[:], rw2T.ap())
        gdma(rb1f[:], rb1_d.ap())
        gdma(atb[:], AT_d.ap().rearrange("(c p) e -> p c e", p=128))
        gdma(btb[:], BT_d.ap().rearrange("(c p) e -> p c e", p=128))
        gdma(bias_b[:], fbias.ap())
        w(GPS, "gs")
        bar()
        VEC.tensor_copy(r1b[:], r1f[:])
        VEC.tensor_tensor(r1l[:], r1f[:], r1b[:], op=ALU.subtract)
        VEC.tensor_copy(rb1_s[:], rb1f[:])
        VEC.tensor_tensor(rb1lo[:], rb1f[:], rb1_s[:], op=ALU.subtract)
        bar()
        for tt in range(NQT):
            xt = xnT_q[:, :, 128 * tt:128 * tt + 128]
            xe = xn_err[:, :, 128 * tt:128 * tt + 128]
            for dc in range(DC):
                TEN.matmul(psh[:, :], xt[:, dc], r1b[:, dc],
                           start=(dc == 0), stop=False)
                TEN.matmul(psh[:, :], xt[:, dc], r1l[:, dc],
                           start=False, stop=False)
                TEN.matmul(psh[:, :], xe[:, dc], r1b[:, dc],
                           start=False, stop=False)
            TEN.matmul(psh[:, :], ones_rb[:], rb1_s[:],
                       start=False, stop=False)
            inc(TEN.matmul(psh[:, :], ones_rb[:], rb1lo[:],
                           start=False, stop=True), "pe")
            w(SCA, "pe")
            SCA.activation(hrow[:], psh[:, :], AF.Silu)
            w(TEN, "ac")
            inc(TEN.transpose(psh2[:, :], hrow[:, :], identF[:]), "pe")
            w(VEC, "pe")
            VEC.tensor_copy(hrowT[:, :], psh2[:, :])
            w(TEN, "dv")
            TEN.matmul(psg[:, :], hrowT[:, :], r2f[:], start=True, stop=False)
            inc(TEN.matmul(psg[:, :], ones_rf[:], brow_f[:],
                           start=False, stop=True), "pe")
            w(VEC, "pe")
            VEC.tensor_copy(logi[:], psg[:, :])
            VEC.reduce_max(m1[:], logi[:], axis=AX.X)
            VEC.tensor_scalar(m1n[:], m1[:], -1.0, None, op0=ALU.mult)
            VEC.tensor_scalar(eq[:], logi[:], m1[:], None, op0=ALU.is_ge)
            VEC.tensor_scalar(eq[:], eq[:], -1e30, None, op0=ALU.mult)
            VEC.tensor_tensor(eq[:], logi[:], eq[:], op=ALU.add)
            VEC.reduce_max(m2[:], eq[:], axis=AX.X)
            w(SCA, "dv")
            SCA.activation(ee[:], logi[:], AF.Exp, bias=m1n[:])
            w(VEC, "ac")
            VEC.tensor_scalar(eq[:], logi[:], m2[:], None, op0=ALU.is_ge)
            VEC.tensor_tensor(ee[:], ee[:], eq[:], op=ALU.mult)
            VEC.reduce_sum(den[:], ee[:], axis=AX.X)
            VEC.reciprocal(den[:], den[:])
            VEC.tensor_scalar(coeff[:], ee[:], den[:], None, op0=ALU.mult)
            for dc in range(DC):
                m = TEN.matmul(psx[:, :], xt[:, dc], atb[:, dc],
                               start=(dc == 0), stop=(dc == DC - 1))
            inc(m, "pe")
            w(VEC, "pe")
            VEC.tensor_copy(xa_s[:], psx[:, :])
            for r in range(RANK):
                VEC.tensor_copy(
                    crep[:].rearrange("p (e r) -> p e r", r=RANK)[:, :, r],
                    coeff[:])
            VEC.tensor_tensor(wxa[:], xa_s[:], crep[:], op=ALU.mult)
            w(TEN, "dv")
            for eh in range(2):
                m = TEN.transpose(pst[:, eh, :], wxa[:, 128 * eh:128 * eh + 128],
                                  ident[:])
            inc(m, "pe")
            w(VEC, "pe")
            VEC.tensor_copy(wxaT_a[:, tt, :, :], pst[:, :, :])
            w(TEN, "dv")
        bar()
        gdma(wfb[:, 0], WT_d.ap()[:, 0:FOC].rearrange("(c p) x -> p c x", p=128),
             s="ga")
        evac_dv = []
        pe_oc = []
        for oc in range(D // FOC):
            if oc + 1 < D // FOC:
                if oc >= 1:
                    w(GPS, "pe", pe_oc[oc - 1])
                gdma(wfb[:, (oc + 1) % 2],
                     WT_d.ap()[:, FOC * (oc + 1):FOC * (oc + 1) + FOC]
                     .rearrange("(c p) x -> p c x", p=128),
                     s=("ga", "gb")[(oc + 1) % 2])
            w(TEN, ("ga", "gb")[oc % 2])
            for tt in range(NQT):
                g = oc * NQT + tt
                p = psfl[g % 2]
                if g >= 2:
                    w(TEN, "dv", evac_dv[g - 2])
                xt = xnT_q[:, :, 128 * tt:128 * tt + 128]
                for dc in range(DC):
                    TEN.matmul(p[:, :], xt[:, dc], wfb[:, oc % 2, dc],
                               start=(dc == 0), stop=False)
                for eh in range(2):
                    TEN.matmul(p[:, :], wxaT_a[:, tt, eh, :],
                               btb[:, eh, FOC * oc:FOC * oc + FOC],
                               start=False, stop=False)
                inc(TEN.matmul(p[:, :], ones_rb[:],
                               bias_b[0:1, FOC * oc:FOC * oc + FOC],
                               start=False, stop=True), "pe")
                w(VEC, "pe")
                VEC.tensor_copy(ffn_s[:, tt, FOC * oc:FOC * oc + FOC], p[:, :])
                evac_dv.append(cnt["dv"])
            pe_oc.append(cnt["pe"])
        bar()
    bar()

    with nc.gpsimd.register("fl") as fl, nc.gpsimd.register("it") as it:
        GPS.reg_mov(fl, 0)
        GPS.reg_mov(it, 0)
        with nc.bb("kv_poll_check"):
            GPS.br_lt(fl, MAGIC, "kv_poll_body", "kv_poll_done")
        with nc.bb("kv_poll_body"):
            GPS.sem_clear(sems["po"])
            with GPS.If(hv == 0):
                GPS.dma_start(flagv[:], FLG.ap()[1:2, :]).then_inc(sems["po"], 16)
            with GPS.Else():
                GPS.dma_start(flagv[:], FLG.ap()[0:1, :]).then_inc(sems["po"], 16)
            GPS.wait_ge(sems["po"], 16)
            GPS.reg_load(fl, flagv[0:1, 0:1])
            GPS.reg_add(it, it, 1)
            GPS.br_lt(it, 2000000, "kv_poll_check", "kv_poll_done")
        with nc.bb("kv_poll_done"):
            GPS.nop()
    bar()

    with contextlib.ExitStack() as st:
        knrow = st.enter_context(nc.sbuf_tensor("s_kn", [H, S], F32))
        kmax_c = st.enter_context(nc.sbuf_tensor("s_kc", [H, 1], F32))
        crow = st.enter_context(nc.sbuf_tensor("s_cr", [1, H], F32))
        crow_b = st.enter_context(nc.sbuf_tensor("s_cb", [1, H], BF16))
        psb2 = st.enter_context(nc.psum_tensor("s_ps", [128, H], F32))
        hdma(knrow[:], KN_g.ap())
        w(VEC, "hs")
        VEC.reduce_max(kmax_c[:, 0:1], knrow[:], axis=AX.X)
        w(GPS, "dv")
        gdma(scr.ap()[0:1, 0:H].rearrange("a h -> h a"), kmax_c[:, 0:1])
        w(GPS, "gs")
        gdma(crow[0:1, 0:H], scr.ap()[0:1, 0:H])
        w(VEC, "gs")
        VEC.tensor_tensor(crow[:], crow[:], qmax_r[:], op=ALU.mult)
        w(SCA, "dv")
        SCA.activation(crow[:], crow[:], AF.Sqrt)
        w(VEC, "ac")
        VEC.tensor_scalar(crow[:], crow[:], -1.0, None, op0=ALU.mult)
        VEC.tensor_copy(crow_b[:], crow[:])
        w(TEN, "dv")
        inc(TEN.matmul(psb2[:, :], ones_rb[:], crow_b[:], start=True, stop=True),
            "pe")
        w(VEC, "pe")
        VEC.tensor_copy(shifts[:], psb2[:, :])
        bar()

    ctx_ctx = contextlib.ExitStack()
    ctxT = ctx_ctx.enter_context(nc.sbuf_tensor("ctxT", [128, DC, QTOK], BF16))
    with contextlib.ExitStack() as st:
        kst = st.enter_context(nc.sbuf_tensor("a_k", [128, 2, S], BF16))
        vst = st.enter_context(nc.sbuf_tensor("a_v", [128, 2, NKC, 128], BF16))
        pT = st.enter_context(nc.sbuf_tensor("a_p", [128, 2, 2, 512], BF16))
        lrec = st.enter_context(nc.sbuf_tensor("a_lr", [1, 512], F32))
        lscr = st.enter_context(nc.sbuf_tensor("a_ls", [1, 512], F32))
        psum2 = st.enter_context(nc.sbuf_tensor("a_p2", [128, 512], BF16))
        lrecb = st.enter_context(nc.sbuf_tensor("a_lb16", [1, 512], BF16))
        lb = st.enter_context(nc.sbuf_tensor("a_lb", [128, 512], F32))
        psS = st.enter_context(nc.psum_tensor("a_s", [128, 2, 2, 512], F32))
        psC = st.enter_context(nc.psum_tensor("a_c", [128, 512], F32))
        psL = st.enter_context(nc.psum_tensor("a_l", [1, 512], F32))
        psB = st.enter_context(nc.psum_tensor("a_b", [128, 512], F32))
        pe_head_end = {}
        strip_cnt = {}

        def load_strips(hh):
            par_ = hh % 2
            sl_ = ("ka", "kb")[par_]
            gdma(kst[:, par_, :], KT_g.ap()[128 * hh:128 * hh + 128, :], s=sl_)
            gdma(vst[:, par_, :, :],
                 V_g.ap()[:, 128 * hh:128 * hh + 128]
                 .rearrange("(kc p) d -> p kc d", p=128), s=sl_)
            strip_cnt[hh] = cnt[sl_]

        load_strips(0)
        for h in range(H):
            par = h % 2
            sl = ("ka", "kb")[par]
            if h + 1 < H:
                if h >= 1:
                    w(GPS, "pe", pe_head_end[h - 1])
                load_strips(h + 1)
            w(TEN, sl, strip_cnt[h])
            w(SCA, sl, strip_cnt[h])
            for qc in range(NQC):
                nkc = min(16 * (qc + 1), NKC)
                dense = nkc - 16
                ac_rounds = []
                psum_dvs = []
                ri = 0
                for kc0 in range(0, nkc, 2):
                    g2 = ri % 2
                    if ri >= 2:
                        w(TEN, "ac", ac_rounds[ri - 2])
                        w(SCA, "dv", psum_dvs[ri - 2])
                    for j, kc in enumerate((kc0, kc0 + 1)):
                        if kc >= nkc:
                            continue
                        qoff = 0 if kc < dense else 128 * ((kc - dense) // 4)
                        m = TEN.matmul(
                            psS[:, g2, j, qoff:],
                            kst[:, par, 128 * kc:128 * kc + 128],
                            qT[:, h, 512 * qc + qoff:512 * qc + 512],
                            start=True, stop=True)
                    inc(m, "pe")
                    w(SCA, "pe")
                    for j, kc in enumerate((kc0, kc0 + 1)):
                        if kc >= nkc:
                            continue
                        qoff = 0 if kc < dense else 128 * ((kc - dense) // 4)
                        SCA.activation(pT[:, g2, j, qoff:], psS[:, g2, j, qoff:],
                                       AF.Exp, bias=shifts[:, h:h + 1])
                    ac_rounds.append(cnt["ac"])
                    w(TEN, "ac")
                    w(VEC, "ac")
                    nm = None
                    for j, kc in enumerate((kc0, kc0 + 1)):
                        if kc >= nkc or kc < dense:
                            continue
                        wk = kc - dense
                        qoff = 128 * (wk // 4)
                        nm = VEC.tensor_tensor(pT[:, g2, j, qoff:qoff + 128],
                                               pT[:, g2, j, qoff:qoff + 128],
                                               masks[:, wk % 4, :], op=ALU.mult)
                    if nm is not None:
                        w(TEN, "dv")
                    qoff0 = 0 if kc0 < dense else 128 * ((kc0 - dense) // 4)
                    VEC.tensor_tensor(psum2[:, qoff0:], pT[:, g2, 0, qoff0:],
                                      pT[:, g2, 1, qoff0:], op=ALU.add)
                    psum_dv = cnt["dv"]
                    psum_dvs.append(psum_dv)
                    for j, kc in enumerate((kc0, kc0 + 1)):
                        if kc >= nkc:
                            continue
                        qoff = 0 if kc < dense else 128 * ((kc - dense) // 4)
                        first, last = (kc == 0), (kc == nkc - 1)
                        TEN.matmul(psC[:, qoff:], vst[:, par, kc, :],
                                   pT[:, g2, j, qoff:], start=first, stop=last)
                    w(TEN, "dv", psum_dv)
                    mL = TEN.matmul(psL[0:1, qoff0:], ones_cb[:],
                                    psum2[:, qoff0:],
                                    start=(kc0 == 0), stop=(kc0 + 1 >= nkc - 1))
                    ri += 1
                inc(mL, "pe")
                w(VEC, "pe")
                VEC.reciprocal(lrec[:], psL[0:1, :])
                w(TEN, "dv")
                inc(TEN.matmul(psB[:, :], ones_rf[:], lrec[:],
                               start=True, stop=True), "pe")
                w(VEC, "pe")
                VEC.tensor_copy(lb[:], psB[:, :])
                VEC.tensor_tensor(ctxT[:, h, 512 * qc:512 * qc + 512],
                                      psC[:, :], lb[:], op=ALU.mult)
                w(TEN, "dv")
            pe_head_end[h] = cnt["pe"]
        bar()
    bar()

    if debug_taps:
        with nc.sbuf_tensor("dbg_t", [128, max(D, QTOK)], F32) as dbt:
            for tt in range(NQT):
                VEC.tensor_copy(dbt[:, 0:D], ffn_s[:, tt, :])
                w(SYN, "dv")
                hdma(dbg_ffn.ap()[128 * tt:128 * tt + 128, :], dbt[:, 0:D])
                w(VEC, "hs")
            for dc in range(DC):
                VEC.tensor_copy(dbt[:, 0:QTOK], ctxT[:, dc, :])
                w(SYN, "dv")
                hdma(dbg_ctx.ap()[128 * dc:128 * dc + 128, :], dbt[:, 0:QTOK])
                w(VEC, "hs")
                VEC.tensor_copy(dbt[:, 0:QTOK], qT[:, dc, :])
                w(SYN, "dv")
                hdma(dbg_qT.ap()[128 * dc:128 * dc + 128, :], dbt[:, 0:QTOK])
                w(VEC, "hs")
            w(SYN, "dv")
            hdma(dbg_shift.ap(), shifts[:])
            w(SYN, "hs")
            bar()

    with contextlib.ExitStack() as st:
        wo = st.enter_context(nc.sbuf_tensor("o_w", [128, 2, DC, OC], BF16))
        xrow = st.enter_context(nc.sbuf_tensor("o_x", [128, 2, OC], F32))
        acc1 = st.enter_context(nc.sbuf_tensor("o_a", [128, OC], F32))
        outt = st.enter_context(nc.sbuf_tensor("o_o", [128, 2, OC], F32))
        pso0 = st.enter_context(nc.psum_tensor("o_p0", [128, OC], F32))
        pso1 = st.enter_context(nc.psum_tensor("o_p1", [128, OC], F32))
        psol = [pso0, pso1]
        gdma(wo[:, 0], outW.ap()[:, 0:OC].rearrange("(c p) x -> p c x", p=128),
             s="ga")
        use_dv = []
        st_c = []
        pe_oc = []
        for oc in range(D // OC):
            if oc + 1 < D // OC:
                if oc >= 1:
                    w(GPS, "pe", pe_oc[oc - 1])
                gdma(wo[:, (oc + 1) % 2],
                     outW.ap()[:, OC * (oc + 1):OC * (oc + 1) + OC]
                     .rearrange("(c p) x -> p c x", p=128),
                     s=("ga", "gb")[(oc + 1) % 2])
            w(TEN, ("ga", "gb")[oc % 2])
            for tt in range(NQT):
                g = oc * NQT + tt
                p = psol[g % 2]
                lsl = ("ha", "hb")[g % 2]
                ssl = ("sa", "sb")[g % 2]
                if g >= 2:
                    w(SYN, "dv", use_dv[g - 2])
                ld = hdma(xrow[:, g % 2, :],
                          x_q.ap()[128 * tt:128 * tt + 128,
                                   OC * oc:OC * oc + OC], s=lsl)
                if g >= 2:
                    w(TEN, "dv", use_dv[g - 2])
                for dc in range(DC):
                    m = TEN.matmul(p[:, :], ctxT[:, dc, 128 * tt:128 * tt + 128],
                                   wo[:, oc % 2, dc],
                                   start=(dc == 0), stop=(dc == DC - 1))
                inc(m, "pe")
                w(VEC, "pe")
                w(VEC, lsl, ld)
                if g >= 2:
                    w(VEC, ssl, st_c[g - 2])
                VEC.tensor_tensor(acc1[:], p[:, :],
                                  ffn_s[:, tt, OC * oc:OC * oc + OC],
                                  op=ALU.add)
                VEC.tensor_tensor(outt[:, g % 2, :], acc1[:],
                                  xrow[:, g % 2, :], op=ALU.add)
                use_dv.append(cnt["dv"])
                w(SYN, "dv")
                st_c.append(
                    hdma(out_d.ap()[128 * tt:128 * tt + 128,
                                    OC * oc:OC * oc + OC],
                         outt[:, g % 2, :], s=ssl))
            pe_oc.append(cnt["pe"])
        w(SYN, "sa")
        w(SYN, "sb")
        w(GPS, "sa")
        w(GPS, "sb")
    bar()
    ctx_ctx.close()
    es.close()
    nc.finalize()
    return nc


_NC_CACHE = {}


def kernel(L0_attn_x, L0_attn_Wqkv, L0_attn_q_ln_scale, L0_attn_k_ln_scale,
           L0_attn_out_W, L0_ffn_W, L0_ffn_bias, L0_ffn_rw1, L0_ffn_rb1,
           L0_ffn_rw2, L0_ffn_rb2, L0_ffn_A, L0_ffn_B, L0_ffn_gates):
    cfg = REAL
    if "nc" not in _NC_CACHE:
        _NC_CACHE["nc"] = build_nc(cfg)
    nc = _NC_CACHE["nc"]
    in_maps = prep_inputs(cfg, L0_attn_x, L0_attn_Wqkv, L0_attn_q_ln_scale,
                          L0_attn_k_ln_scale, L0_attn_out_W, L0_ffn_W,
                          L0_ffn_bias, L0_ffn_rw1, L0_ffn_rb1, L0_ffn_rw2,
                          L0_ffn_rb2, L0_ffn_A, L0_ffn_B, L0_ffn_gates)
    res = run_bass_kernel_spmd(nc, in_maps, core_ids=list(range(NCORES)))
    out = np.empty((cfg.B, cfg.S, cfg.D), np.float32)
    for c in range(NCORES):
        b, cp = c // 4, c % 4
        out[b, cp::4, :] = res.results[c]["out"]
    return out


# version 13
# speedup vs baseline: 1.4867x; 1.4151x over previous
import sys
sys.path.insert(0, "/opt/trn_rl_repo")
import contextlib
import numpy as np
import concourse.bass as bass
import concourse.mybir as mybir
from concourse.bass_utils import run_bass_kernel_spmd

F32 = mybir.dt.float32
BF16 = mybir.dt.bfloat16
I32 = mybir.dt.int32
AF = mybir.ActivationFunctionType
ALU = mybir.AluOpType
AX = mybir.AxisListType

NCORES = 8
MAGIC = 1337


class Cfg:
    def __init__(self, B=2, S=4096, D=2048, H=16):
        self.B, self.S, self.D, self.H = B, S, D, H
        self.HD = D // H
        assert self.HD == 128 and S % 2048 == 0
        self.OC = min(512, D)
        self.DC = D // 128
        self.QTOK = S // 4
        self.KTOK = S // 2
        self.NQC = self.QTOK // 512
        self.NKC = S // 128
        self.NQT = self.QTOK // 128
        self.RANK, self.NE, self.RH = 16, 16, 64
        self.ER = self.RANK * self.NE


REAL = Cfg()


def make_masks(cprime):
    import ml_dtypes
    kk = np.arange(128)[:, None]
    jj = np.arange(128)[None, :]
    m = np.stack([(128 * w4 + kk <= 4 * jj + cprime) for w4 in range(4)])
    return np.ascontiguousarray(m.astype(ml_dtypes.bfloat16))


def prep_inputs(cfg, x, Wqkv, q_ln_scale, k_ln_scale, out_W,
                W, bias, rw1, rb1, rw2, rb2, A, B_, gates):
    B, S, D = cfg.B, cfg.S, cfg.D
    x = np.asarray(x)
    A = np.asarray(A); B_ = np.asarray(B_)
    AT = np.ascontiguousarray(A.transpose(2, 0, 1).reshape(D, cfg.ER))
    BT = np.ascontiguousarray(B_.transpose(0, 2, 1).reshape(cfg.ER, D))
    WT = np.ascontiguousarray(np.asarray(W).T)
    rw1T = np.ascontiguousarray(np.asarray(rw1).T)
    rw2T = np.ascontiguousarray(np.asarray(rw2).T[:, :cfg.NE])
    import ml_dtypes
    ident = np.eye(128, dtype=ml_dtypes.bfloat16)
    in_maps = []
    for c in range(NCORES):
        b, cp = c // 4, c % 4
        kvh = cp % 2
        qidx = np.arange(cp, S, 4)
        kvs = slice(kvh * (S // 2), (kvh + 1) * (S // 2))
        xb = x[b]
        in_maps.append({
            "x_qT": np.ascontiguousarray(xb[qidx].T),
            "x_kvT": np.ascontiguousarray(xb[kvs].T),
            "x_q": np.ascontiguousarray(xb[qidx]),
            "Wqkv": np.asarray(Wqkv),
            "outW": np.asarray(out_W),
            "WT": WT,
            "rw1T": rw1T,
            "rw2T": rw2T,
            "AT": AT,
            "BT": BT,
            "qls": np.asarray(q_ln_scale).reshape(1, D),
            "kls": np.asarray(k_ln_scale).reshape(1, D),
            "fbias": np.asarray(bias).reshape(1, D),
            "rb1": np.asarray(rb1).reshape(1, cfg.RH),
            "rb2": np.ascontiguousarray(np.asarray(rb2)[:cfg.NE].reshape(1, cfg.NE)),
            "gates": np.asarray(gates).reshape(1, cfg.NE),
            "masks": make_masks(cp),
            "ident": ident,
            "kvhalf": np.array([[kvh]], np.int32),
        })
    return in_maps


def build_nc(cfg, detect_races=False, debug_taps=False):
    nc = bass.Bass("TRN2", target_bir_lowering=False, num_devices=NCORES,
                   detect_race_conditions=detect_races,
                   dynamic_dma_scratch_size=16384)
    D, DC, H, S = cfg.D, cfg.DC, cfg.H, cfg.S
    QTOK, KTOK, NQC, NKC, NQT = cfg.QTOK, cfg.KTOK, cfg.NQC, cfg.NKC, cfg.NQT
    ER, NE, RH, RANK = cfg.ER, cfg.NE, cfg.RH, cfg.RANK
    OC = cfg.OC

    es = contextlib.ExitStack()
    sems, cnt = {}, {}

    def sem(n):
        sems[n] = es.enter_context(nc.semaphore(n))
        cnt[n] = 0

    for n in ("hs", "gs", "pe", "dv", "ac", "po", "ga", "gb", "ha", "hb", "sa", "sb", "ka", "kb", "va", "vb", "pb", "ua", "ub", "uc", "ud", "na", "nb"):
        sem(n)

    def inc(instr, n, v=1):
        instr.then_inc(sems[n], v)
        cnt[n] += v
        return cnt[n]

    def hdma(dst, src, s="hs"):
        return inc(nc.sync.dma_start(dst, src), s, 16)

    def gdma(dst, src, cond=None, s="gs"):
        if cond is None:
            return inc(nc.gpsimd.dma_start(dst, src), s, 16)
        return inc(nc.gpsimd.dma_start(dst, src, cond=cond), s, 16)

    def w(engine, n, v=None):
        engine.wait_ge(sems[n], cnt[n] if v is None else v)

    def gdma_nc(dst, src, s="gs"):
        nc.gpsimd.dma_start(dst, src).then_inc(sems[s], 16)

    bar = nc.all_engine_barrier

    _WRAP = {"tensor_tensor", "tensor_scalar", "tensor_copy", "reduce_max",
             "reduce_sum", "reciprocal", "memset", "tensor_single_scalar",
             "activation", "select", "tensor_add", "tensor_mul", "tensor_sub"}

    class _Proxy:
        def __init__(self, eng, s):
            self._e, self._s = eng, s

        def __getattr__(self, name):
            attr = getattr(self._e, name)
            if name in _WRAP:
                def f(*a, **k):
                    self._e.wait_ge(sems[self._s], cnt[self._s])
                    r = attr(*a, **k)
                    r.then_inc(sems[self._s], 1)
                    cnt[self._s] += 1
                    return r
                return f
            return attr

    TEN, GPS, SYN = nc.tensor, nc.gpsimd, nc.sync
    VEC = _Proxy(nc.vector, "dv")
    SCA = _Proxy(nc.scalar, "ac")

    dt = lambda n, s, d=F32, kind="ExternalInput", **kw: \
        nc.dram_tensor(n, s, d, kind=kind, **kw)

    x_qT = dt("x_qT", [D, QTOK])
    x_kvT = dt("x_kvT", [D, KTOK])
    x_q = dt("x_q", [QTOK, D])
    Wqkv = dt("Wqkv", [D, 3 * D])
    outW = dt("outW", [D, D])
    WT_d = dt("WT", [D, D])
    rw1T = dt("rw1T", [D, RH])
    rw2T = dt("rw2T", [RH, NE])
    AT_d = dt("AT", [D, ER])
    BT_d = dt("BT", [ER, D])
    qls_d = dt("qls", [1, D])
    kls_d = dt("kls", [1, D])
    fbias = dt("fbias", [1, D])
    rb1_d = dt("rb1", [1, RH])
    rb2_d = dt("rb2", [1, NE])
    gates_d = dt("gates", [1, NE])
    masks_d = dt("masks", [4, 128, 128], BF16)
    ident_d = dt("ident", [128, 128], BF16)
    kvhalf_d = dt("kvhalf", [1, 1], I32)
    out_d = dt("out", [QTOK, D], kind="ExternalOutput")
    if debug_taps:
        dbg_ffn = dt("dbg_ffn", [NQT * 128, D], kind="ExternalOutput")
        dbg_ctx = dt("dbg_ctx", [D, QTOK], kind="ExternalOutput")
        dbg_qT = dt("dbg_qT", [D, QTOK], kind="ExternalOutput")
        dbg_shift = dt("dbg_shift", [128, H], kind="ExternalOutput")

    KT_g = dt("KT_g", [D, S], BF16, kind="Internal", addr_space="Shared")
    V_g = dt("V_g", [S, D], BF16, kind="Internal", addr_space="Shared")
    KN_g = dt("KN_g", [H, S], F32, kind="Internal", addr_space="Shared")
    FLG = dt("FLG", [2, 1], I32, kind="Internal", addr_space="Shared")
    scr = dt("scr", [1, 64], F32, kind="Internal")
    KT_l = dt("KT_l", [D, KTOK], BF16, kind="Internal")
    V_l = dt("V_l", [KTOK, D], BF16, kind="Internal")
    KN_l = dt("KN_l", [H, KTOK], F32, kind="Internal")
    xnq_d = dt("xnq_d", [D, QTOK], BF16, kind="Internal")
    xne_d = dt("xne_d", [D, QTOK], BF16, kind="Internal")

    qT = es.enter_context(nc.sbuf_tensor("qT", [128, DC, QTOK], BF16))
    ones_cb = es.enter_context(nc.sbuf_tensor("ones_cb", [128, 1], BF16))
    ones_rb = es.enter_context(nc.sbuf_tensor("ones_rb", [1, 128], BF16))
    ones_cf = es.enter_context(nc.sbuf_tensor("ones_cf", [128, 1], F32))
    ident = es.enter_context(nc.sbuf_tensor("ident_s", [128, 128], BF16))
    masks = es.enter_context(nc.sbuf_tensor("masks_s", [128, 4, 128], BF16))
    kvh_s = es.enter_context(nc.sbuf_tensor("kvh_s", [1, 1], I32))
    brow = es.enter_context(nc.sbuf_tensor("brow", [1, NE], BF16))
    rb1_s = es.enter_context(nc.sbuf_tensor("rb1_s", [1, RH], BF16))
    shifts = es.enter_context(nc.sbuf_tensor("shifts", [128, H], F32))
    qmax_r = es.enter_context(nc.sbuf_tensor("qmax_r", [1, H], F32))
    flagv = es.enter_context(nc.sbuf_tensor("flagv", [1, 1], I32))
    magic_s = es.enter_context(nc.sbuf_tensor("magic_s", [1, 1], I32))
    zero_s = es.enter_context(nc.sbuf_tensor("zero_s", [1, 1], I32))
    srow2 = es.enter_context(nc.sbuf_tensor("srow2", [1, 2 * NE], F32))
    eps5 = es.enter_context(nc.sbuf_tensor("eps5", [1, 1], F32))
    ones_rf = es.enter_context(nc.sbuf_tensor("ones_rf", [1, 128], F32))
    identF = es.enter_context(nc.sbuf_tensor("identF", [128, 128], F32))
    brow_f = es.enter_context(nc.sbuf_tensor("brow_f", [1, NE], F32))
    eps6 = es.enter_context(nc.sbuf_tensor("eps6", [1, 1], F32))

    gdma(kvh_s[:], kvhalf_d.ap())
    gdma(masks[:], masks_d.ap().rearrange("w p q -> p w q"))
    gdma(ident[:], ident_d.ap())
    gdma(srow2[0:1, 0:NE], rb2_d.ap())
    gdma(srow2[0:1, NE:2 * NE], gates_d.ap())
    gdma(rb1_s[:], rb1_d.ap())
    w(GPS, "gs")
    GPS.memset(ones_cb[:], 1.0)
    GPS.memset(ones_rb[:], 1.0)
    GPS.memset(ones_cf[:], 1.0)
    GPS.memset(ones_rf[:], 1.0)
    GPS.memset(magic_s[:], MAGIC)
    GPS.memset(eps5[:], 1e-5)
    GPS.memset(eps6[:], 1e-6)
    GPS.memset(zero_s[:], 0)
    hv = GPS.value_load(kvh_s[0:1, 0:1])
    bar()
    with GPS.If(hv == 0):
        gdma(FLG.ap()[0:1, :], zero_s[:])
    with GPS.Else():
        gdma_nc(FLG.ap()[1:2, :], zero_s[:])
    w(GPS, "gs")
    VEC.tensor_tensor(brow_f[:], srow2[0:1, 0:NE], srow2[0:1, NE:2 * NE],
                      op=ALU.add)
    VEC.tensor_copy(identF[:], ident[:])
    bar()

    def ln_fm(xsrc, t0, T, dst, eps_t, tag, dst_err=None, stash=None,
              resident=False, end_bar=True):
        NT = T // 512
        NS = DC if resident else 4
        PF = 4
        with contextlib.ExitStack() as st:
            xa = st.enter_context(nc.sbuf_tensor(f"lnx_{tag}", [128, NS, T], F32))
            xsq = st.enter_context(nc.sbuf_tensor(f"lnq_{tag}", [128, T], F32))
            mrow = st.enter_context(nc.sbuf_tensor(f"lnm_{tag}", [1, T], F32))
            vrow = st.enter_context(nc.sbuf_tensor(f"lnv_{tag}", [1, T], F32))
            trow = st.enter_context(nc.sbuf_tensor(f"lnt_{tag}", [1, T], F32))
            mrow_b = st.enter_context(nc.sbuf_tensor(f"lnmb_{tag}", [1, T], BF16))
            rrow_b = st.enter_context(nc.sbuf_tensor(f"lnrb_{tag}", [1, T], BF16))
            mb = st.enter_context(nc.sbuf_tensor(f"lnM_{tag}", [128, T], BF16))
            rb = st.enter_context(nc.sbuf_tensor(f"lnR_{tag}", [128, T], BF16))
            ps_s = st.enter_context(nc.psum_tensor(f"lns_{tag}", [1, T], F32))
            ps_q = st.enter_context(nc.psum_tensor(f"lnp_{tag}", [128, T], F32))
            ps_b = st.enter_context(nc.psum_tensor(f"lnw_{tag}", [128, 2, 512], F32))
            pe0 = cnt["pe"]
            dvs0 = cnt["dv"]
            ld = {}

            def loadx(dc, slot_guard_pe=None, slot_guard_dv=None):
                sl_ = ("ha", "hb")[dc % 2]
                if slot_guard_pe is not None:
                    w(SYN, "pe", slot_guard_pe)
                if slot_guard_dv is not None:
                    w(SYN, "dv", slot_guard_dv)
                hdma(xa[:, dc % NS, :],
                     xsrc.ap()[:, t0:t0 + T].rearrange("(c p) t -> c p t",
                                                       p=128)[dc], s=sl_)
                ld[dc] = cnt[sl_]

            for dc in range(min(PF, DC)):
                loadx(dc)
            for dc in range(DC):
                sl = ("ha", "hb")[dc % 2]
                w(TEN, sl, ld[dc])
                w(SCA, sl, ld[dc])
                if dc > 0:
                    w(SCA, "pe", pe0 + dc)
                SCA.activation(xsq[:], xa[:, dc % NS, :], AF.Square)
                if stash is not None:
                    w(VEC, sl, ld[dc])
                    VEC.tensor_copy(stash[:, dc, :], xa[:, dc % NS, :])
                w(TEN, "ac")
                for t in range(NT):
                    TEN.matmul(ps_s[0:1, 512 * t:512 * t + 512], ones_cf[:],
                               xa[:, dc % NS, 512 * t:512 * t + 512],
                               start=(dc == 0), stop=(dc == DC - 1))
                    m = TEN.matmul(ps_q[64:65, 512 * t:512 * t + 512], ones_cf[:],
                                   xsq[:, 512 * t:512 * t + 512],
                                   start=(dc == 0), stop=(dc == DC - 1),
                                   tile_position=(0, 64))
                    if t == NT - 1:
                        inc(m, "pe")
                if dc + PF < DC:
                    if resident:
                        loadx(dc + PF)
                    else:
                        loadx(dc + PF, slot_guard_pe=pe0 + dc + 1,
                              slot_guard_dv=(cnt["dv"] if stash is not None
                                             else None))
            bar()
            VEC.tensor_scalar(mrow[:], ps_s[0:1, :], 1.0 / D, None, op0=ALU.mult)
            VEC.tensor_scalar(vrow[:], ps_q[64:65, :], 1.0 / D, None, op0=ALU.mult)
            VEC.tensor_tensor(trow[:], mrow[:], mrow[:], op=ALU.mult)
            VEC.tensor_tensor(vrow[:], vrow[:], trow[:], op=ALU.subtract)
            w(SCA, "dv")
            SCA.activation(vrow[:], vrow[:], AF.Sqrt, bias=eps_t[0:1, :])
            w(VEC, "ac")
            VEC.reciprocal(vrow[:], vrow[:])
            VEC.tensor_copy(mrow_b[:], mrow[:])
            VEC.tensor_copy(rrow_b[:], vrow[:])
            w(TEN, "dv")
            for bi, (rsrc, dstb) in enumerate(((mrow_b, mb), (rrow_b, rb))):
                for t in range(NT):
                    m = TEN.matmul(ps_b[:, t % 2, :], ones_rb[:],
                                   rsrc[0:1, 512 * t:512 * t + 512],
                                   start=True, stop=True)
                    if t % 2 == 1 or t == NT - 1:
                        inc(m, "pe")
                        w(VEC, "pe")
                        t0b = t - (t % 2)
                        VEC.tensor_copy(
                            dstb[:, 512 * t0b:512 * (t + 1)],
                            ps_b[:, 0:(t % 2) + 1, :])
                        w(TEN, "dv")
            bar()
            src_dv = []
            if resident:
                for dc in range(DC):
                    VEC.tensor_tensor(xsq[:], xa[:, dc, :], mb[:],
                                      op=ALU.subtract)
                    if dst_err is None:
                        VEC.tensor_tensor(dst[:, dc, :], xsq[:], rb[:],
                                          op=ALU.mult)
                    else:
                        VEC.tensor_tensor(xsq[:], xsq[:], rb[:], op=ALU.mult)
                        VEC.tensor_copy(dst[:, dc, :], xsq[:])
                        VEC.tensor_tensor(dst_err[:, dc, :], xsq[:],
                                          dst[:, dc, :], op=ALU.subtract)
                    src_dv.append(cnt["dv"])
            elif stash is not None:
                for dc in range(DC):
                    VEC.tensor_tensor(xsq[:], stash[:, dc, :], mb[:],
                                      op=ALU.subtract)
                    VEC.tensor_tensor(dst[:, dc, :], xsq[:], rb[:], op=ALU.mult)
                    src_dv.append(cnt["dv"])
            else:
                dv0 = cnt["dv"]
                opd = 2 if dst_err is None else 4
                ld2 = {}

                def loadx2(dc):
                    sl_ = ("ha", "hb")[dc % 2]
                    if dc >= NS:
                        w(SYN, "dv", dv0 + opd * (dc - NS + 1))
                    hdma(xa[:, dc % NS, :],
                         xsrc.ap()[:, t0:t0 + T].rearrange("(c p) t -> c p t",
                                                           p=128)[dc], s=sl_)
                    ld2[dc] = cnt[sl_]

                for dc in range(min(NS, DC)):
                    loadx2(dc)
                for dc in range(DC):
                    sl = ("ha", "hb")[dc % 2]
                    w(VEC, sl, ld2[dc])
                    VEC.tensor_tensor(xsq[:], xa[:, dc % NS, :], mb[:],
                                      op=ALU.subtract)
                    if dst_err is None:
                        VEC.tensor_tensor(dst[:, dc, :], xsq[:], rb[:],
                                          op=ALU.mult)
                    else:
                        VEC.tensor_tensor(xsq[:], xsq[:], rb[:], op=ALU.mult)
                        VEC.tensor_copy(dst[:, dc, :], xsq[:])
                        VEC.tensor_tensor(dst_err[:, dc, :], xsq[:],
                                          dst[:, dc, :], op=ALU.subtract)
                    src_dv.append(cnt["dv"])
                    if dc + NS < DC:
                        loadx2(dc + NS)
            if end_bar:
                bar()
            return src_dv

    def proj_ln(cb, xn_src, T, scale_dram, eps_t, tag, sink, nrm_sink=None,
                rstd_scale=None, src_dv=None, wb_ext=None, scale_ext=None,
                direct_out=None):
        NT = T // 512
        with contextlib.ExitStack() as st:
            wb = wb_ext if wb_ext is not None else st.enter_context(
                nc.sbuf_tensor(f"pw_{tag}", [128, 2, DC, 128], BF16))
            scrow = scale_ext if scale_ext is not None else st.enter_context(
                nc.sbuf_tensor(f"pscr_{tag}", [DC, 128], F32))
            scol = st.enter_context(nc.sbuf_tensor(f"psc_{tag}", [128, DC], F32))
            acc = st.enter_context(nc.sbuf_tensor(f"pa_{tag}", [128, DC, T], BF16))
            outc = st.enter_context(nc.sbuf_tensor(f"po_{tag}", [128, 4, T], BF16))
            sq = st.enter_context(nc.sbuf_tensor(f"pq_{tag}", [128, 2, T], BF16))
            mrow = st.enter_context(nc.sbuf_tensor(f"pm_{tag}", [1, T], F32))
            vrow = st.enter_context(nc.sbuf_tensor(f"pv_{tag}", [1, T], F32))
            mrow_b = st.enter_context(nc.sbuf_tensor(f"pmb_{tag}", [1, T], BF16))
            rrow_b = st.enter_context(nc.sbuf_tensor(f"prb_{tag}", [1, T], BF16))
            mb = st.enter_context(nc.sbuf_tensor(f"pM_{tag}", [128, T], BF16))
            rsb = st.enter_context(nc.sbuf_tensor(f"pR_{tag}", [128, 2, T], BF16))
            nrow = st.enter_context(nc.sbuf_tensor(f"pn_{tag}", [1, 2, T], F32))
            ps0 = st.enter_context(nc.psum_tensor(f"p0_{tag}", [128, 512], F32))
            ps1 = st.enter_context(nc.psum_tensor(f"p1_{tag}", [128, 512], F32))
            ps_s = st.enter_context(nc.psum_tensor(f"p2_{tag}", [1, T], F32))
            ps_q = st.enter_context(nc.psum_tensor(f"p3_{tag}", [1, T], F32))
            ps_b = st.enter_context(nc.psum_tensor(f"p4_{tag}", [128, 512], F32))
            psl = [ps0, ps1]
            pe_cc = []
            stats_pe = {}
            square_ac = {}
            evac_dvs = {}
            gdma(wb[:, 0], Wqkv.ap()[:, cb:cb + 128]
                 .rearrange("(c p) x -> p c x", p=128), s="ga")
            gdma(scrow[:], scale_dram.ap().rearrange("a (c p) -> (a c) p",
                                                     p=128), s="gs")
            dv0, pe0 = cnt["dv"], cnt["pe"]
            gtile = 0

            def emit_square(j):
                w(SCA, "dv", evac_dvs[j])
                if j >= 2:
                    w(SCA, "pe", stats_pe[j - 2])
                SCA.activation(sq[:, j % 2, :], acc[:, j, :], AF.Square)
                square_ac[j] = cnt["ac"]

            def emit_stats(j):
                w(TEN, "ac", square_ac[j])
                for t in range(NT):
                    TEN.matmul(ps_s[0:1, 512 * t:512 * t + 512], ones_cb[:],
                               acc[:, j, 512 * t:512 * t + 512],
                               start=(j == 0), stop=(j == DC - 1))
                    m = TEN.matmul(ps_q[0:1, 512 * t:512 * t + 512], ones_cb[:],
                                   sq[:, j % 2, 512 * t:512 * t + 512],
                                   start=(j == 0), stop=(j == DC - 1))
                inc(m, "pe")
                stats_pe[j] = cnt["pe"]

            for cc in range(DC):
                if cc + 1 < DC:
                    if cc >= 1:
                        w(GPS, "pe", pe_cc[cc - 1])
                    gdma(wb[:, (cc + 1) % 2],
                         Wqkv.ap()[:, cb + 128 * (cc + 1):cb + 128 * (cc + 1) + 128]
                         .rearrange("(c p) x -> p c x", p=128),
                         s=("ga", "gb")[(cc + 1) % 2])
                w(TEN, ("ga", "gb")[cc % 2])
                for t in range(NT):
                    p = psl[gtile % 2]
                    if gtile >= 2:
                        w(TEN, "dv", dv0 + gtile - 1)
                    for dc in range(DC):
                        if src_dv is not None and cc == 0 and t == 0:
                            w(TEN, "dv", src_dv[dc])
                        m = TEN.matmul(p[:, :], wb[:, cc % 2, dc],
                                       xn_src[:, dc, 512 * t:512 * t + 512],
                                       start=(dc == 0), stop=(dc == DC - 1))
                    inc(m, "pe")
                    w(VEC, "pe")
                    VEC.tensor_copy(acc[:, cc, 512 * t:512 * t + 512], p[:, :])
                    gtile += 1
                evac_dvs[cc] = cnt["dv"]
                pe_cc.append(cnt["pe"])
                emit_square(cc)
                if cc >= 1:
                    emit_stats(cc - 1)
            emit_stats(DC - 1)
            bar()
            VEC.tensor_scalar(mrow[:], ps_s[0:1, :], 1.0 / D, None, op0=ALU.mult)
            VEC.tensor_scalar(vrow[:], ps_q[0:1, :], 1.0 / D, None, op0=ALU.mult)
            VEC.tensor_tensor(nrow[0:1, 0, :], mrow[:], mrow[:], op=ALU.mult)
            VEC.tensor_tensor(vrow[:], vrow[:], nrow[0:1, 0, :], op=ALU.subtract)
            w(SCA, "dv")
            SCA.activation(vrow[:], vrow[:], AF.Sqrt, bias=eps_t[0:1, :])
            w(VEC, "ac")
            VEC.reciprocal(vrow[:], vrow[:])
            if rstd_scale is not None:
                VEC.tensor_scalar(vrow[:], vrow[:], float(rstd_scale), None,
                                  op0=ALU.mult)
            VEC.tensor_copy(mrow_b[:], mrow[:])
            VEC.tensor_copy(rrow_b[:], vrow[:])
            w(TEN, "dv")
            for rsrc, dstb in ((mrow_b, mb[:, :]), (rrow_b, rsb[:, 0, :])):
                for t in range(NT):
                    inc(TEN.matmul(ps_b[:, :], ones_rb[:],
                                   rsrc[0:1, 512 * t:512 * t + 512],
                                   start=True, stop=True), "pe")
                    w(VEC, "pe")
                    VEC.tensor_copy(dstb[:, 512 * t:512 * t + 512], ps_b[:, :])
                    w(TEN, "dv")
            bar()
            w(TEN, "gs")
            inc(TEN.transpose(ps_b[:, 0:DC], scrow[:], identF[0:DC, 0:DC]), "pe")
            w(VEC, "pe")
            VEC.tensor_copy(scol[:], ps_b[:, 0:DC])
            ac_dc = {}
            US = ("ua", "ub", "uc", "ud")
            NS_ = ("na", "nb")
            dv_dc = {}
            for dc in range(DC):
                pp = dc % 2
                sl4 = dc % 4
                if dc >= 2:
                    w(VEC, "ac", ac_dc[dc - 2])
                VEC.tensor_tensor(sq[:, pp, :], acc[:, dc, :], mb[:],
                                  op=ALU.subtract)
                VEC.tensor_tensor(sq[:, pp, :], sq[:, pp, :], rsb[:, 0, :],
                                  op=ALU.mult)
                dv_dc[dc] = cnt["dv"]
                w(SCA, "dv", dv_dc[dc])
                if direct_out is not None:
                    SCA.activation(direct_out(dc), sq[:, pp, :], AF.Copy,
                                   scale=scol[:, dc:dc + 1])
                else:
                    if dc >= 4:
                        w(SCA, US[sl4])
                    SCA.activation(outc[:, sl4, :], sq[:, pp, :], AF.Copy,
                                   scale=scol[:, dc:dc + 1])
                if nrm_sink is not None:
                    SCA.activation(acc[:, dc, :], sq[:, pp, :], AF.Square,
                                   scale=scol[:, dc:dc + 1])
                ac_dc[dc] = cnt["ac"]
                if direct_out is None:
                    sink(dc, outc[:, sl4, :], US[sl4])
            if nrm_sink is not None:
                psn = [ps_s, ps_q]
                evac_dv = {}
                for dc in range(DC):
                    pp = dc % 2
                    w(TEN, "ac", ac_dc[dc])
                    if dc >= 2:
                        w(TEN, "dv", evac_dv[dc - 2])
                    for t in range(NT):
                        m = TEN.matmul(psn[pp][0:1, 512 * t:512 * t + 512],
                                       ones_cb[:],
                                       acc[:, dc, 512 * t:512 * t + 512],
                                       start=True, stop=True)
                    inc(m, "pe")
                    if dc >= 2:
                        w(VEC, NS_[pp])
                    w(VEC, "pe")
                    VEC.tensor_copy(nrow[0:1, pp, :], psn[pp][0:1, :])
                    evac_dv[dc] = cnt["dv"]
                    nrm_sink(dc, nrow[0:1, pp, :], NS_[pp])
            for s_ in US + NS_:
                if cnt[s_]:
                    w(GPS, s_)
            bar()

    xnq_ctx = contextlib.ExitStack()
    xnT_q = xnq_ctx.enter_context(nc.sbuf_tensor("xnT_q", [128, DC, QTOK], BF16))
    xn_err = xnq_ctx.enter_context(nc.sbuf_tensor("xn_err", [128, DC, QTOK], BF16))
    wbq = xnq_ctx.enter_context(nc.sbuf_tensor("wbq", [128, 2, DC, 128], BF16))
    scq = xnq_ctx.enter_context(nc.sbuf_tensor("scq", [DC, 128], F32))
    q_dv = ln_fm(x_qT, 0, QTOK, xnT_q, eps5, "q", dst_err=xn_err,
                 resident=True, end_bar=False)

    def q_sink(dc, tile, sem):
        VEC.tensor_copy(qT[:, dc, :], tile)

    def q_nrm(dc, row, sem):
        VEC.reduce_max(qmax_r[0:1, dc:dc + 1], row, axis=AX.X)

    proj_ln(0, xnT_q, QTOK, qls_d, eps6, "q", q_sink, q_nrm,
            rstd_scale=1.0 / float(np.sqrt(128.0)), src_dv=q_dv,
            wb_ext=wbq, scale_ext=scq,
            direct_out=lambda dc: qT[:, dc, :])
    bar()

    gdma(xnq_d.ap().rearrange("(c p) t -> p c t", p=128), xnT_q[:])
    gdma(xne_d.ap().rearrange("(c p) t -> p c t", p=128), xn_err[:])
    w(GPS, "gs")
    bar()
    xnq_ctx.close()

    TKV = KTOK // 2
    with contextlib.ExitStack() as kv_ctx:
        xnT_kv = kv_ctx.enter_context(
            nc.sbuf_tensor("xnT_kv", [128, DC, TKV], BF16))
        xstash = kv_ctx.enter_context(nc.sbuf_tensor("xstash", [128, DC, KTOK // 2],
                                                     BF16))
        VOC = min(256, OC)
        vw = kv_ctx.enter_context(nc.sbuf_tensor("vw", [128, 2, DC, VOC], BF16))
        vout = kv_ctx.enter_context(nc.sbuf_tensor("vout", [128, 2, VOC], BF16))
        for sh in range(2):
            t0 = sh * TKV
            ln_fm(x_kvT, t0, TKV, xnT_kv, eps5, f"kv{sh}", stash=xstash)

            def k_sink(dc, tile, sem, _t0=t0):
                w(GPS, "ac")
                gdma(KT_l.ap()[128 * dc:128 * dc + 128, _t0:_t0 + TKV], tile,
                     s=sem)

            def k_nrm(dc, row, sem, _t0=t0):
                w(GPS, "dv")
                gdma(KN_l.ap()[dc:dc + 1, _t0:_t0 + TKV], row, s=sem)

            proj_ln(D, xnT_kv, TKV, kls_d, eps6, f"k{sh}", k_sink, k_nrm)
            bar()
            NKT = TKV // 128
            with contextlib.ExitStack() as st:
                psv0 = st.enter_context(nc.psum_tensor(f"v0_{sh}", [128, VOC], F32))
                psv1 = st.enter_context(nc.psum_tensor(f"v1_{sh}", [128, VOC], F32))
                psvl = [psv0, psv1]
                gdma(vw[:, 0], Wqkv.ap()[:, 2 * D:2 * D + VOC]
                     .rearrange("(c p) x -> p c x", p=128), s="ga")
                evac_dv = []
                store_gs = []
                pe_cc = []
                for cc in range(D // VOC):
                    if cc + 1 < D // VOC:
                        if cc >= 1:
                            w(GPS, "pe", pe_cc[cc - 1])
                        gdma(vw[:, (cc + 1) % 2],
                             Wqkv.ap()[:, 2 * D + VOC * (cc + 1):
                                       2 * D + VOC * (cc + 1) + VOC]
                             .rearrange("(c p) x -> p c x", p=128),
                             s=("ga", "gb")[(cc + 1) % 2])
                    w(TEN, ("ga", "gb")[cc % 2])
                    for tt in range(NKT):
                        g = cc * NKT + tt
                        p = psvl[g % 2]
                        if g >= 2:
                            w(TEN, "dv", evac_dv[g - 2])
                        for dc in range(DC):
                            m = TEN.matmul(p[:, :],
                                           xnT_kv[:, dc, 128 * tt:128 * tt + 128],
                                           vw[:, cc % 2, dc],
                                           start=(dc == 0), stop=(dc == DC - 1))
                        inc(m, "pe")
                        w(VEC, "pe")
                        if g >= 2:
                            w(VEC, ("va", "vb")[g % 2], store_gs[g - 2])
                        VEC.tensor_copy(vout[:, g % 2, :], p[:, :])
                        evac_dv.append(cnt["dv"])
                        w(GPS, "dv")
                        gdma(V_l.ap()[t0 + 128 * tt:t0 + 128 * tt + 128,
                                      VOC * cc:VOC * cc + VOC],
                             vout[:, g % 2, :], s=("va", "vb")[g % 2])
                        store_gs.append(cnt[("va", "vb")[g % 2]])
                    pe_cc.append(cnt["pe"])
                w(GPS, "va")
                w(GPS, "vb")
                bar()
    bar()
    w(GPS, "gs")
    w(GPS, "ga")
    w(GPS, "gb")
    w(GPS, "va")
    w(GPS, "vb")
    PR = min(512, D)
    PV = min(1024, KTOK)
    hv2 = nc.sync.value_load(kvh_s[0:1, 0:1])

    def publish(hh, emit):
        for rr in range(0, D, PR):
            emit(KT_g.ap()[rr:rr + PR, hh * KTOK:(hh + 1) * KTOK],
                 KT_l.ap()[rr:rr + PR, :])
        for rr in range(0, KTOK, PV):
            emit(V_g.ap()[hh * KTOK + rr: hh * KTOK + rr + PV, :],
                 V_l.ap()[rr:rr + PV, :])
        emit(KN_g.ap()[:, hh * KTOK:(hh + 1) * KTOK], KN_l.ap())

    w(SYN, "gs")
    w(SYN, "va")
    w(SYN, "vb")
    with nc.sync.If(hv2 == 0):
        publish(0, lambda d_, s_: hdma(d_, s_, s="pb"))
    with nc.sync.Else():
        publish(1, lambda d_, s_: nc.sync.dma_start(d_, s_).then_inc(sems["pb"], 16))
    w(SYN, "pb")
    with nc.sync.If(hv2 == 0):
        hdma(FLG.ap()[0:1, :], magic_s[:], s="pb")
    with nc.sync.Else():
        nc.sync.dma_start(FLG.ap()[1:2, :], magic_s[:]).then_inc(sems["pb"], 16)

    ffn_ctx = contextlib.ExitStack()
    ffn_s = ffn_ctx.enter_context(nc.sbuf_tensor("ffn_s", [128, NQT, D], BF16))
    with contextlib.ExitStack() as st:
        wxaT_a = st.enter_context(nc.sbuf_tensor("wxaT_a", [128, NQT, 2, 128], BF16))
        xnT_q = st.enter_context(nc.sbuf_tensor("xnq2", [128, DC, QTOK], BF16))
        xn_err = st.enter_context(nc.sbuf_tensor("xne2", [128, DC, QTOK], BF16))
        gdma(xnT_q[:], xnq_d.ap().rearrange("(c p) t -> p c t", p=128))
        gdma(xn_err[:], xne_d.ap().rearrange("(c p) t -> p c t", p=128))
        r1f = st.enter_context(nc.sbuf_tensor("f_r1f", [128, DC, RH], F32))
        r1b = st.enter_context(nc.sbuf_tensor("f_r1", [128, DC, RH], BF16))
        r1l = st.enter_context(nc.sbuf_tensor("f_r1l", [128, DC, RH], BF16))
        r2f = st.enter_context(nc.sbuf_tensor("f_r2", [RH, NE], F32))
        rb1f = st.enter_context(nc.sbuf_tensor("f_rb1f", [1, RH], F32))
        rb1lo = st.enter_context(nc.sbuf_tensor("f_rb1lo", [1, RH], BF16))
        atb = st.enter_context(nc.sbuf_tensor("f_at", [128, DC, ER], BF16))
        btb = st.enter_context(nc.sbuf_tensor("f_bt", [128, 2, D], BF16))
        bias_b = st.enter_context(nc.sbuf_tensor("f_bias", [1, D], BF16))
        hrow = st.enter_context(nc.sbuf_tensor("f_h", [128, RH], F32))
        hrowT = st.enter_context(nc.sbuf_tensor("f_hT", [RH, 128], F32))
        logi = st.enter_context(nc.sbuf_tensor("f_lg", [128, NE], F32))
        m1 = st.enter_context(nc.sbuf_tensor("f_m1", [128, 1], F32))
        m1n = st.enter_context(nc.sbuf_tensor("f_m1n", [128, 1], F32))
        m2 = st.enter_context(nc.sbuf_tensor("f_m2", [128, 1], F32))
        eq = st.enter_context(nc.sbuf_tensor("f_eq", [128, NE], F32))
        ee = st.enter_context(nc.sbuf_tensor("f_ee", [128, NE], F32))
        den = st.enter_context(nc.sbuf_tensor("f_den", [128, 1], F32))
        coeff = st.enter_context(nc.sbuf_tensor("f_cf", [128, NE], F32))
        crep = st.enter_context(nc.sbuf_tensor("f_cr", [128, ER], BF16))
        xa_s = st.enter_context(nc.sbuf_tensor("f_xa", [128, ER], F32))
        wxa = st.enter_context(nc.sbuf_tensor("f_wxa", [128, ER], BF16))
        FOC = min(256, OC)
        wfb = st.enter_context(nc.sbuf_tensor("f_w", [128, 2, DC, FOC], BF16))
        psh = st.enter_context(nc.psum_tensor("f_ph", [128, RH], F32))
        psg = st.enter_context(nc.psum_tensor("f_pg", [128, NE], F32))
        psx = st.enter_context(nc.psum_tensor("f_px", [128, ER], F32))
        pst = st.enter_context(nc.psum_tensor("f_pt", [128, 2, 128], BF16))
        psh2 = st.enter_context(nc.psum_tensor("f_ph2", [RH, 128], F32))
        psf0 = st.enter_context(nc.psum_tensor("f_p0", [128, 256 if OC >= 256 else OC], F32))
        psf1 = st.enter_context(nc.psum_tensor("f_p1", [128, 256 if OC >= 256 else OC], F32))
        psfl = [psf0, psf1]
        gdma(r1f[:], rw1T.ap().rearrange("(c p) r -> p c r", p=128))
        gdma(r2f[:], rw2T.ap())
        gdma(rb1f[:], rb1_d.ap())
        gdma(atb[:], AT_d.ap().rearrange("(c p) e -> p c e", p=128))
        gdma(btb[:], BT_d.ap().rearrange("(c p) e -> p c e", p=128))
        gdma(bias_b[:], fbias.ap())
        w(GPS, "gs")
        bar()
        VEC.tensor_copy(r1b[:], r1f[:])
        VEC.tensor_tensor(r1l[:], r1f[:], r1b[:], op=ALU.subtract)
        VEC.tensor_copy(rb1_s[:], rb1f[:])
        VEC.tensor_tensor(rb1lo[:], rb1f[:], rb1_s[:], op=ALU.subtract)
        bar()
        for tt in range(NQT):
            xt = xnT_q[:, :, 128 * tt:128 * tt + 128]
            xe = xn_err[:, :, 128 * tt:128 * tt + 128]
            for dc in range(DC):
                TEN.matmul(psh[:, :], xt[:, dc], r1b[:, dc],
                           start=(dc == 0), stop=False)
                TEN.matmul(psh[:, :], xt[:, dc], r1l[:, dc],
                           start=False, stop=False)
                TEN.matmul(psh[:, :], xe[:, dc], r1b[:, dc],
                           start=False, stop=False)
            TEN.matmul(psh[:, :], ones_rb[:], rb1_s[:],
                       start=False, stop=False)
            inc(TEN.matmul(psh[:, :], ones_rb[:], rb1lo[:],
                           start=False, stop=True), "pe")
            w(SCA, "pe")
            SCA.activation(hrow[:], psh[:, :], AF.Silu)
            w(TEN, "ac")
            inc(TEN.transpose(psh2[:, :], hrow[:, :], identF[:]), "pe")
            w(VEC, "pe")
            VEC.tensor_copy(hrowT[:, :], psh2[:, :])
            w(TEN, "dv")
            TEN.matmul(psg[:, :], hrowT[:, :], r2f[:], start=True, stop=False)
            inc(TEN.matmul(psg[:, :], ones_rf[:], brow_f[:],
                           start=False, stop=True), "pe")
            w(VEC, "pe")
            for dc in range(DC):
                m = TEN.matmul(psx[:, :], xt[:, dc], atb[:, dc],
                               start=(dc == 0), stop=(dc == DC - 1))
            inc(m, "pe")
            VEC.tensor_copy(logi[:], psg[:, :])
            VEC.reduce_max(m1[:], logi[:], axis=AX.X)
            VEC.tensor_scalar(m1n[:], m1[:], -1.0, None, op0=ALU.mult)
            VEC.tensor_scalar(eq[:], logi[:], m1[:], None, op0=ALU.is_ge)
            VEC.tensor_scalar(eq[:], eq[:], -1e30, None, op0=ALU.mult)
            VEC.tensor_tensor(eq[:], logi[:], eq[:], op=ALU.add)
            VEC.reduce_max(m2[:], eq[:], axis=AX.X)
            w(SCA, "dv")
            SCA.activation(ee[:], logi[:], AF.Exp, bias=m1n[:])
            w(VEC, "ac")
            VEC.tensor_scalar(eq[:], logi[:], m2[:], None, op0=ALU.is_ge)
            VEC.tensor_tensor(ee[:], ee[:], eq[:], op=ALU.mult)
            VEC.reduce_sum(den[:], ee[:], axis=AX.X)
            VEC.reciprocal(den[:], den[:])
            VEC.tensor_scalar(coeff[:], ee[:], den[:], None, op0=ALU.mult)
            w(VEC, "pe")
            VEC.tensor_copy(xa_s[:], psx[:, :])
            for r in range(RANK):
                VEC.tensor_copy(
                    crep[:].rearrange("p (e r) -> p e r", r=RANK)[:, :, r],
                    coeff[:])
            VEC.tensor_tensor(wxa[:], xa_s[:], crep[:], op=ALU.mult)
            w(TEN, "dv")
            for eh in range(2):
                m = TEN.transpose(pst[:, eh, :], wxa[:, 128 * eh:128 * eh + 128],
                                  ident[:])
            inc(m, "pe")
            w(VEC, "pe")
            VEC.tensor_copy(wxaT_a[:, tt, :, :], pst[:, :, :])
            w(TEN, "dv")
        bar()
        gdma(wfb[:, 0], WT_d.ap()[:, 0:FOC].rearrange("(c p) x -> p c x", p=128),
             s="ga")
        evac_dv = []
        pe_oc = []
        for oc in range(D // FOC):
            if oc + 1 < D // FOC:
                if oc >= 1:
                    w(GPS, "pe", pe_oc[oc - 1])
                gdma(wfb[:, (oc + 1) % 2],
                     WT_d.ap()[:, FOC * (oc + 1):FOC * (oc + 1) + FOC]
                     .rearrange("(c p) x -> p c x", p=128),
                     s=("ga", "gb")[(oc + 1) % 2])
            w(TEN, ("ga", "gb")[oc % 2])
            for tt in range(NQT):
                g = oc * NQT + tt
                p = psfl[g % 2]
                if g >= 2:
                    w(TEN, "dv", evac_dv[g - 2])
                xt = xnT_q[:, :, 128 * tt:128 * tt + 128]
                for dc in range(DC):
                    TEN.matmul(p[:, :], xt[:, dc], wfb[:, oc % 2, dc],
                               start=(dc == 0), stop=False)
                for eh in range(2):
                    TEN.matmul(p[:, :], wxaT_a[:, tt, eh, :],
                               btb[:, eh, FOC * oc:FOC * oc + FOC],
                               start=False, stop=False)
                inc(TEN.matmul(p[:, :], ones_rb[:],
                               bias_b[0:1, FOC * oc:FOC * oc + FOC],
                               start=False, stop=True), "pe")
                w(VEC, "pe")
                VEC.tensor_copy(ffn_s[:, tt, FOC * oc:FOC * oc + FOC], p[:, :])
                evac_dv.append(cnt["dv"])
            pe_oc.append(cnt["pe"])
        bar()
    bar()

    with nc.gpsimd.register("fl") as fl, nc.gpsimd.register("it") as it:
        GPS.reg_mov(fl, 0)
        GPS.reg_mov(it, 0)
        with nc.bb("kv_poll_check"):
            GPS.br_lt(fl, MAGIC, "kv_poll_body", "kv_poll_done")
        with nc.bb("kv_poll_body"):
            with nc.gpsimd.register("dly") as dly:
                GPS.reg_mov(dly, 0)
                for _ in range(400):
                    GPS.reg_add(dly, dly, 1)
            GPS.sem_clear(sems["po"])
            with GPS.If(hv == 0):
                GPS.dma_start(flagv[:], FLG.ap()[1:2, :]).then_inc(sems["po"], 16)
            with GPS.Else():
                GPS.dma_start(flagv[:], FLG.ap()[0:1, :]).then_inc(sems["po"], 16)
            GPS.wait_ge(sems["po"], 16)
            GPS.reg_load(fl, flagv[0:1, 0:1])
            GPS.reg_add(it, it, 1)
            GPS.br_lt(it, 2000000, "kv_poll_check", "kv_poll_done")
        with nc.bb("kv_poll_done"):
            GPS.nop()
    bar()

    with contextlib.ExitStack() as st:
        knrow = st.enter_context(nc.sbuf_tensor("s_kn", [H, S], F32))
        kmax_c = st.enter_context(nc.sbuf_tensor("s_kc", [H, 1], F32))
        crow = st.enter_context(nc.sbuf_tensor("s_cr", [1, H], F32))
        crow_b = st.enter_context(nc.sbuf_tensor("s_cb", [1, H], BF16))
        psb2 = st.enter_context(nc.psum_tensor("s_ps", [128, H], F32))
        hdma(knrow[:], KN_g.ap())
        w(VEC, "hs")
        VEC.reduce_max(kmax_c[:, 0:1], knrow[:], axis=AX.X)
        w(GPS, "dv")
        gdma(scr.ap()[0:1, 0:H].rearrange("a h -> h a"), kmax_c[:, 0:1])
        w(GPS, "gs")
        gdma(crow[0:1, 0:H], scr.ap()[0:1, 0:H])
        w(VEC, "gs")
        VEC.tensor_tensor(crow[:], crow[:], qmax_r[:], op=ALU.mult)
        w(SCA, "dv")
        SCA.activation(crow[:], crow[:], AF.Sqrt)
        w(VEC, "ac")
        VEC.tensor_scalar(crow[:], crow[:], -1.0, None, op0=ALU.mult)
        VEC.tensor_copy(crow_b[:], crow[:])
        w(TEN, "dv")
        inc(TEN.matmul(psb2[:, :], ones_rb[:], crow_b[:], start=True, stop=True),
            "pe")
        w(VEC, "pe")
        VEC.tensor_copy(shifts[:], psb2[:, :])
        bar()

    ctx_ctx = contextlib.ExitStack()
    ctxT = ctx_ctx.enter_context(nc.sbuf_tensor("ctxT", [128, DC, QTOK], BF16))
    with contextlib.ExitStack() as st:
        kst = st.enter_context(nc.sbuf_tensor("a_k", [128, 2, S], BF16))
        vst = st.enter_context(nc.sbuf_tensor("a_v", [128, 2, NKC, 128], BF16))
        pT = st.enter_context(nc.sbuf_tensor("a_p", [128, 2, 2, 512], BF16))
        lrec = st.enter_context(nc.sbuf_tensor("a_lr", [1, 512], F32))
        psum2 = st.enter_context(nc.sbuf_tensor("a_p2", [128, 2, 512], BF16))
        lb = st.enter_context(nc.sbuf_tensor("a_lb", [128, 512], F32))
        psS = st.enter_context(nc.psum_tensor("a_s", [128, 2, 2, 512], F32))
        psC = st.enter_context(nc.psum_tensor("a_c", [128, 2, 512], F32))
        psLB = st.enter_context(nc.psum_tensor("a_lb2", [128, 2, 512], F32))
        pe_head_end = {}
        strip_cnt = {}

        def load_strips(hh):
            par_ = hh % 2
            sl_ = ("ka", "kb")[par_]
            gdma(kst[:, par_, :], KT_g.ap()[128 * hh:128 * hh + 128, :], s=sl_)
            gdma(vst[:, par_, :, :],
                 V_g.ap()[:, 128 * hh:128 * hh + 128]
                 .rearrange("(kc p) d -> p kc d", p=128), s=sl_)
            strip_cnt[hh] = cnt[sl_]

        load_strips(0)
        exp_ac = {}
        ctx_pe = {}
        vec_dv = {}
        sc_pe = {}
        last_epi_dv = {0: 0, 1: 0}
        last_recip_ac = {0: 0, 1: 0}
        gri = 0
        pending_epi = [None]

        def flush_epi():
            if pending_epi[0] is not None:
                pending_epi[0]()
                pending_epi[0] = None

        for h in range(H):
            par = h % 2
            sl = ("ka", "kb")[par]
            if h + 1 < H:
                if h >= 1:
                    w(GPS, "pe", pe_head_end[h - 1])
                load_strips(h + 1)
            w(TEN, sl, strip_cnt[h])
            w(SCA, sl, strip_cnt[h])
            for qc in range(NQC):
                nkc = min(16 * (qc + 1), NKC)
                dense = nkc - 16
                slot = (h * NQC + qc) % 2
                R = nkc // 2
                g0 = gri

                def mk_qoff(ri):
                    kc0 = 2 * ri
                    return 0 if kc0 < dense else 128 * ((kc0 - dense) // 4)

                def emit_scores(ri, _h=h, _qc=qc, _par=par, _g0=g0,
                                _dense=dense):
                    g = _g0 + ri
                    g2 = g % 2
                    if g >= 2 and (g - 2) in exp_ac:
                        w(TEN, "ac", exp_ac[g - 2])
                    for j in range(2):
                        kc = 2 * ri + j
                        qoff = 0 if kc < _dense else 128 * ((kc - _dense) // 4)
                        m = TEN.matmul(
                            psS[:, g2, j, qoff:],
                            kst[:, _par, 128 * kc:128 * kc + 128],
                            qT[:, _h, 512 * _qc + qoff:512 * _qc + 512],
                            start=True, stop=True)
                    inc(m, "pe")
                    sc_pe[g] = cnt["pe"]

                def emit_exp(ri, _h=h, _g0=g0, _dense=dense):
                    g = _g0 + ri
                    g2 = g % 2
                    w(SCA, "pe", sc_pe[g])
                    if (g - 2) in ctx_pe:
                        w(SCA, "pe", ctx_pe[g - 2])
                    if (g - 2) in vec_dv:
                        w(SCA, "dv", vec_dv[g - 2])
                    kc0 = 2 * ri
                    qoff = 0 if kc0 < _dense else 128 * ((kc0 - _dense) // 4)
                    SCA.activation(pT[:, g2, :, qoff:], psS[:, g2, :, qoff:],
                                   AF.Exp, bias=shifts[:, _h:_h + 1])
                    exp_ac[g] = cnt["ac"]

                def emit_vec(ri, _g0=g0, _dense=dense):
                    g = _g0 + ri
                    g2 = g % 2
                    w(VEC, "ac", exp_ac[g])
                    for j in range(2):
                        kc = 2 * ri + j
                        if kc < _dense:
                            continue
                        wk = kc - _dense
                        qoff = 128 * (wk // 4)
                        VEC.tensor_tensor(pT[:, g2, j, qoff:qoff + 128],
                                          pT[:, g2, j, qoff:qoff + 128],
                                          masks[:, wk % 4, :], op=ALU.mult)
                    qoff0 = mk_qoff(ri)
                    VEC.tensor_tensor(psum2[:, g2, qoff0:], pT[:, g2, 0, qoff0:],
                                      pT[:, g2, 1, qoff0:], op=ALU.add)
                    vec_dv[g] = cnt["dv"]

                def emit_ctx(ri, _g0=g0, _dense=dense, _par=par, _slot=slot,
                             _R=R):
                    g = _g0 + ri
                    g2 = g % 2
                    if ri == 0:
                        w(TEN, "dv", last_epi_dv[_slot])
                        w(TEN, "ac", last_recip_ac[_slot])
                    w(TEN, "ac", exp_ac[g])
                    w(TEN, "dv", vec_dv[g])
                    for j in range(2):
                        kc = 2 * ri + j
                        qoff = 0 if kc < _dense else 128 * ((kc - _dense) // 4)
                        TEN.matmul(psC[:, _slot, qoff:], vst[:, _par, kc, :],
                                   pT[:, g2, j, qoff:],
                                   start=(kc == 0), stop=(ri == _R - 1 and j == 1))
                    qoff0 = mk_qoff(ri)
                    mL = TEN.matmul(psLB[0:1, _slot, qoff0:], ones_cb[:],
                                    psum2[:, g2, qoff0:],
                                    start=(ri == 0), stop=(ri == _R - 1))
                    inc(mL, "pe")
                    ctx_pe[g] = cnt["pe"]

                def emit_epilogue(_h=h, _qc=qc, _slot=slot, _g0=g0, _R=R):
                    w(SCA, "pe", ctx_pe[_g0 + _R - 1])
                    SCA.activation(lrec[:], psLB[0:1, _slot, :], AF.Ln)
                    SCA.activation(lrec[:], lrec[:], AF.Exp, scale=-1.0)
                    last_recip_ac[_slot] = cnt["ac"]
                    w(TEN, "ac", last_recip_ac[_slot])
                    inc(TEN.matmul(psLB[:, _slot, :], ones_rf[:], lrec[:],
                                   start=True, stop=True), "pe")
                    w(VEC, "pe")
                    VEC.tensor_copy(lb[:], psLB[:, _slot, :])
                    VEC.tensor_tensor(ctxT[:, _h, 512 * _qc:512 * _qc + 512],
                                      psC[:, _slot, :], lb[:], op=ALU.mult)
                    last_epi_dv[_slot] = cnt["dv"]

                for step in range(R + 2):
                    if step < R:
                        emit_scores(step)
                    if step == 1:
                        flush_epi()
                    if 0 <= step - 1 < R:
                        emit_exp(step - 1)
                        emit_vec(step - 1)
                    if step - 2 >= 0:
                        emit_ctx(step - 2)
                gri += R
                pending_epi[0] = emit_epilogue
            pe_head_end[h] = cnt["pe"]
        flush_epi()
        bar()
    bar()

    if debug_taps:
        with nc.sbuf_tensor("dbg_t", [128, max(D, QTOK)], F32) as dbt:
            for tt in range(NQT):
                VEC.tensor_copy(dbt[:, 0:D], ffn_s[:, tt, :])
                w(SYN, "dv")
                hdma(dbg_ffn.ap()[128 * tt:128 * tt + 128, :], dbt[:, 0:D])
                w(VEC, "hs")
            for dc in range(DC):
                VEC.tensor_copy(dbt[:, 0:QTOK], ctxT[:, dc, :])
                w(SYN, "dv")
                hdma(dbg_ctx.ap()[128 * dc:128 * dc + 128, :], dbt[:, 0:QTOK])
                w(VEC, "hs")
                VEC.tensor_copy(dbt[:, 0:QTOK], qT[:, dc, :])
                w(SYN, "dv")
                hdma(dbg_qT.ap()[128 * dc:128 * dc + 128, :], dbt[:, 0:QTOK])
                w(VEC, "hs")
            w(SYN, "dv")
            hdma(dbg_shift.ap(), shifts[:])
            w(SYN, "hs")
            bar()

    with contextlib.ExitStack() as st:
        wo = st.enter_context(nc.sbuf_tensor("o_w", [128, 2, DC, OC], BF16))
        xrow = st.enter_context(nc.sbuf_tensor("o_x", [128, 2, OC], F32))
        acc1 = st.enter_context(nc.sbuf_tensor("o_a", [128, OC], F32))
        outt = st.enter_context(nc.sbuf_tensor("o_o", [128, 2, OC], F32))
        pso0 = st.enter_context(nc.psum_tensor("o_p0", [128, OC], F32))
        pso1 = st.enter_context(nc.psum_tensor("o_p1", [128, OC], F32))
        psol = [pso0, pso1]
        gdma(wo[:, 0], outW.ap()[:, 0:OC].rearrange("(c p) x -> p c x", p=128),
             s="ga")
        use_dv = []
        st_c = []
        pe_oc = []
        for oc in range(D // OC):
            if oc + 1 < D // OC:
                if oc >= 1:
                    w(GPS, "pe", pe_oc[oc - 1])
                gdma(wo[:, (oc + 1) % 2],
                     outW.ap()[:, OC * (oc + 1):OC * (oc + 1) + OC]
                     .rearrange("(c p) x -> p c x", p=128),
                     s=("ga", "gb")[(oc + 1) % 2])
            w(TEN, ("ga", "gb")[oc % 2])
            for tt in range(NQT):
                g = oc * NQT + tt
                p = psol[g % 2]
                lsl = ("ha", "hb")[g % 2]
                ssl = ("sa", "sb")[g % 2]
                if g >= 2:
                    w(SYN, "dv", use_dv[g - 2])
                ld = hdma(xrow[:, g % 2, :],
                          x_q.ap()[128 * tt:128 * tt + 128,
                                   OC * oc:OC * oc + OC], s=lsl)
                if g >= 2:
                    w(TEN, "dv", use_dv[g - 2])
                for dc in range(DC):
                    m = TEN.matmul(p[:, :], ctxT[:, dc, 128 * tt:128 * tt + 128],
                                   wo[:, oc % 2, dc],
                                   start=(dc == 0), stop=(dc == DC - 1))
                inc(m, "pe")
                w(VEC, "pe")
                w(VEC, lsl, ld)
                if g >= 2:
                    w(VEC, ssl, st_c[g - 2])
                VEC.tensor_tensor(acc1[:], p[:, :],
                                  ffn_s[:, tt, OC * oc:OC * oc + OC],
                                  op=ALU.add)
                VEC.tensor_tensor(outt[:, g % 2, :], acc1[:],
                                  xrow[:, g % 2, :], op=ALU.add)
                use_dv.append(cnt["dv"])
                w(SYN, "dv")
                st_c.append(
                    hdma(out_d.ap()[128 * tt:128 * tt + 128,
                                    OC * oc:OC * oc + OC],
                         outt[:, g % 2, :], s=ssl))
            pe_oc.append(cnt["pe"])
        w(SYN, "sa")
        w(SYN, "sb")
        w(GPS, "sa")
        w(GPS, "sb")
    bar()
    ctx_ctx.close()
    ffn_ctx.close()
    es.close()
    nc.finalize()
    return nc


_NC_CACHE = {}


def kernel(L0_attn_x, L0_attn_Wqkv, L0_attn_q_ln_scale, L0_attn_k_ln_scale,
           L0_attn_out_W, L0_ffn_W, L0_ffn_bias, L0_ffn_rw1, L0_ffn_rb1,
           L0_ffn_rw2, L0_ffn_rb2, L0_ffn_A, L0_ffn_B, L0_ffn_gates):
    cfg = REAL
    if "nc" not in _NC_CACHE:
        _NC_CACHE["nc"] = build_nc(cfg)
    nc = _NC_CACHE["nc"]
    in_maps = prep_inputs(cfg, L0_attn_x, L0_attn_Wqkv, L0_attn_q_ln_scale,
                          L0_attn_k_ln_scale, L0_attn_out_W, L0_ffn_W,
                          L0_ffn_bias, L0_ffn_rw1, L0_ffn_rb1, L0_ffn_rw2,
                          L0_ffn_rb2, L0_ffn_A, L0_ffn_B, L0_ffn_gates)
    res = run_bass_kernel_spmd(nc, in_maps, core_ids=list(range(NCORES)))
    out = np.empty((cfg.B, cfg.S, cfg.D), np.float32)
    for c in range(NCORES):
        b, cp = c // 4, c % 4
        out[b, cp::4, :] = res.results[c]["out"]
    return out
